# version 08
# baseline (speedup 1.0000x reference)
import numpy as np
import ml_dtypes
import concourse.bass as bass
import concourse.mybir as mybir
from concourse.bass_utils import run_bass_kernel_spmd
from contextlib import ExitStack

F32 = mybir.dt.float32
BF16 = mybir.dt.bfloat16
AF = mybir.ActivationFunctionType
ALU = mybir.AluOpType
BFD = ml_dtypes.bfloat16

B, T, D, NH, HD, HID = 2, 2048, 2048, 32, 64, 8192
NTT, NCH, QC = 16, 4, 512
SHIFT = 8.0
SCL = 0.125
GSC = float(1.0 / np.sqrt(HID))

ENGS = ("sync", "tensor", "vector", "scalar", "gpsimd")


class Sem:
    __slots__ = ("h", "n", "key", "eng")

    def __init__(self, h, key):
        self.h, self.n, self.key = h, 0, key
        self.eng = None

    def bump(self, k):
        self.n += k
        return self.n


class Plan:
    def __init__(self):
        self.ops = {e: [] for e in ENGS}
        self.hwm = {}

    def emit(self, eng, fn):
        self.ops[eng].append(fn)

    def wait(self, eng, tok):
        if tok is None:
            return
        sem, val = tok
        if val <= 0:
            return
        k = (eng, sem.key)
        if self.hwm.get(k, -1) >= val:
            return
        self.hwm[k] = val
        self.emit(eng, lambda e, s=sem.h, v=val: e.wait_ge(s, v))

    def run(self, blk):
        for eng in ENGS:
            ops_ = self.ops[eng]

            def mk(o):
                def body(e):
                    for f in o:
                        f(e)
                return body
            getattr(blk, eng)(mk(ops_))


class Bank:
    def __init__(self, ap):
        self.ap = ap
        self.wr = None
        self.rds = []

    def gw(self, p, eng):
        for t in self.rds:
            p.wait(eng, t)
        self.rds = []

    def gr(self, p, eng):
        p.wait(eng, self.wr)


def build_nc():
    nc = bass.Bass("TRN2", target_bir_lowering=False)
    p = Plan()
    es = ExitStack()

    def din(name, shape, dt=F32):
        return nc.dram_tensor(name, shape, dt, kind="ExternalInput").ap()

    x_in = din("x", [B, T, D])
    w_in = {n: din(n, [D, D]) for n in ("wq", "wk", "wv", "wo")}
    bo_in = din("bo", [1, D])
    w_in["w1"] = din("w1", [D, HID])
    w_in["w2"] = din("w2", [HID, D])
    w_in["w3"] = din("w3", [D, HID])
    cost_in = din("cost_t", [128, T])
    sint_in = din("sint_t", [128, T])
    tri_in = din("tri_t", [128, 128], BF16)
    hmat_in = din("hmat_t", [128, 128], BF16)
    out_t = nc.dram_tensor("out", [B, T, D], F32, kind="ExternalOutput").ap()

    def scr(name, shape, dt=BF16):
        return nc.dram_tensor(name, shape, dt, kind="Internal").ap()

    w_bf = {n: scr(n + "_bf", [4, 128, 16 * 512])
            for n in ("wq", "wk", "wv", "wo")}
    w13img = scr("w13img", [32, 128, 2 * 16 * 256])
    w2f64 = scr("w2f64", [4, 128, 64, 512])
    w2h = scr("w2h", [4, 64, 128, 512])
    xn1_s = scr("xn1_s", [B, T, D])
    xn2_s = scr("xn2_s", [B, T, D])
    kT_s = scr("kT_s", [B, D, T])
    va_s = scr("va_s", [B, T, D])
    x1_s = scr("x1_s", [B, T, D], F32)
    z_s = scr("z_s", [2, 64, 128, 512])

    sems = {}
    for n in ("spe", "ssc", "sdv", "sgp"):
        sems[n] = Sem(es.enter_context(nc.semaphore(n)), n)

    def lane(name, eng):
        s = Sem(es.enter_context(nc.semaphore(name)), name)
        s.eng = eng
        sems[name] = s
        return s

    lts = [lane("lts0", "sync"), lane("lts1", "sync")]
    lini = lane("lini", "scalar")
    lini2 = lane("lini2", "scalar")
    lslab = lane("lslab", "scalar")
    lfsl = [lane("lfsl0", "scalar"), lane("lfsl1", "scalar")]
    lw2b = [lane("lw2b0", "scalar"), lane("lw2b1", "scalar")]
    lzld = [lane("lzld0", "scalar"), lane("lzld1", "scalar")]
    lxb = [lane("lxb0", "scalar"), lane("lxb1", "scalar")]
    lx1b = [lane("lx1b0", "scalar"), lane("lx1b1", "scalar")]
    lx1r = lane("lx1r", "scalar")
    lstx = lane("lstx", "gpsimd")
    lsty = lane("lsty", "gpsimd")
    lktl = [lane("lktl0", "gpsimd"), lane("lktl1", "gpsimd")]
    lval = [lane("lval0", "gpsimd"), lane("lval1", "gpsimd")]
    lx1s = [lane("lx1s0", "gpsimd"), lane("lx1s1", "gpsimd")]
    lout = [lane("lout0", "gpsimd"), lane("lout1", "gpsimd")]
    lkb = [lane("lkb0", "gpsimd"), lane("lkb1", "gpsimd")]
    lvb = [lane("lvb0", "gpsimd"), lane("lvb1", "gpsimd")]
    lz = [lane("lz0", "gpsimd"), lane("lz1", "gpsimd")]
    lcast = {n: lane("lc_" + n, "gpsimd")
             for n in ("wk", "wv", "wq", "wo", "w13")}
    lfld = [lane("lfld0", "gpsimd"), lane("lfld1", "gpsimd")]
    lfs = [lane("lfs0", "sync"), lane("lfs1", "sync")]
    lw2h = lane("lw2h", "sync")
    spe, ssc, sdv = sems["spe"], sems["ssc"], sems["sdv"]
    ESEM = {"tensor": spe, "scalar": ssc, "vector": sdv, "gpsimd": sems["sgp"]}

    def dma(q, out_ap, in_ap):
        v = q.bump(16)
        p.emit(q.eng, lambda e, o=out_ap, i=in_ap, sh=q.h:
               e.dma_start(o, i).then_inc(sh, 16))
        return (q, v)

    def dmat(q, out_ap, in_ap):
        v = q.bump(16)
        p.emit("sync", lambda e, o=out_ap, i=in_ap, sh=q.h:
               e.dma_start_transpose(o, i).then_inc(sh, 16))
        return (q, v)

    def op(eng, fn, inc=False):
        if inc:
            s = ESEM[eng]
            v = s.bump(1)
            p.emit(eng, lambda e, f=fn, sh=s.h: f(e).then_inc(sh, 1))
            return (s, v)
        p.emit(eng, fn)
        return None

    def wt(eng, tok):
        if isinstance(tok, list):
            for t in tok:
                p.wait(eng, t)
        else:
            p.wait(eng, tok)

    def barrier():
        toks = [(s, s.n) for s in sems.values()]
        for eng in ENGS:
            for t in toks:
                p.wait(eng, t)

    def sb(name, shape, dt=F32):
        return es.enter_context(nc.sbuf_tensor(name, shape, dt))

    cpool = sb("cpool", [128, 4])
    hmat = sb("hmat", [128, 128], BF16)
    onesb = sb("onesb", [1, 128], BF16)
    costb = sb("costb", [128, T], BF16)
    sintb = sb("sintb", [128, T], BF16)
    tri4 = sb("tri4", [128, 512], BF16)
    BO = sb("BO", [128, D])
    statb = sb("statb", [128, 64])
    xnt = [sb("xnt0", [128, 16 * 512], BF16), sb("xnt1", [128, 16 * 512],
                                                 BF16)]
    slab = sb("slab", [128, 16 * 512], BF16)
    QT = [sb(f"QT{i}", [128, 512], BF16) for i in range(2)]
    prow = [sb(f"prow{i}", [128, 512], BF16) for i in range(2)]
    ropec = sb("ropec", [128, 512], BF16)
    OT = sb("OT", [128, 16 * 512], BF16)
    ebuf = sb("ebuf", [128, 2 * 6 * 512], BF16)
    kbuf = [sb(f"kbuf{i}", [128, 2048], BF16) for i in range(2)]
    vbuf = [sb(f"vbuf{i}", [128, 16 * 130], BF16) for i in range(2)]
    oub = [sb(f"oub{i}", [64, 512], BF16) for i in range(2)]
    rbf = [sb(f"rbf{i}", [1, 512], BF16) for i in range(2)]
    x1p = [sb(f"x1p{i}", [128, 512], F32) for i in range(2)]
    x1r = sb("x1r", [128, D], F32)
    xb = [sb(f"xb{i}", [128, 512], F32) for i in range(2)]
    xnst = sb("xnst", [128, D], BF16)
    fsl = [sb(f"fsl{i}", [128, 16 * 512], BF16) for i in range(2)]
    gbuf = [sb(f"gbuf{i}", [128, 512], BF16) for i in range(2)]
    zst = [sb(f"zst{i}", [128, 512], BF16) for i in range(2)]
    w2b = [sb(f"w2b{i}", [128, 4 * 512], BF16) for i in range(2)]
    zld = [sb(f"zld{i}", [128, 4 * 512], BF16) for i in range(2)]
    x1b = [sb(f"x1b{i}", [128, 512], F32) for i in range(2)]
    x2b = [sb(f"x2b{i}", [128, 512], F32) for i in range(2)]

    pb = [Bank(es.enter_context(nc.psum_tensor(f"pb{i}", [128, 512], F32)))
          for i in range(8)]

    U = {"xnt": [None, None], "slab": None, "xb": [None, None],
         "xnst": None, "prow": [None, None], "qt": [None, None],
         "stat": [None, None], "vstg": [None, None], "x1p": [None, None],
         "x1r": None}
    FU = {"fsl": [None, None], "gbuf": [None, None], "zst": [None, None],
          "w2b": [None, None], "zld": [None, None], "x1b": [None, None],
          "x2b": [None, None], "xnt2": None, "zuser": [None, None]}
    AU = {"kbuf": [None, None], "vbuf": [None, None], "kb": [None, None],
          "vb": [None, None], "vo": [None, None],
          "ebuf": {}, "oub": [None, None], "rbf": [None, None]}

    op("vector", lambda e: e.memset(cpool[:, 0:1], -SHIFT))
    op("vector", lambda e: e.memset(cpool[:, 1:2], 1e-5))
    t_ones = op("vector", lambda e: e.memset(onesb[0:1, :], 1.0), inc=True)
    nc.const_aps.register(F32, -SHIFT, cpool[:, 0:1])
    nc.const_aps.register(F32, 1e-5, cpool[:, 1:2])

    tk = dma(lini, x1r[:], cost_in)
    wt("vector", tk)
    t_cos = op("vector", lambda e: e.tensor_copy(costb[:], x1r[:]), inc=True)
    wt("scalar", t_cos)
    tk = dma(lini, x1r[:], sint_in)
    wt("vector", tk)
    t_sin = op("vector", lambda e: e.tensor_copy(sintb[:], x1r[:]), inc=True)
    wt("scalar", t_sin)
    U["x1r"] = t_sin
    tk = dma(lini, tri4[:, 0:128], tri_in)
    wt("vector", tk)
    t_tri = None
    for r in range(1, 4):
        wt("vector", t_tri)
        t_tri = op("vector", lambda e, r=r: e.tensor_copy(
            tri4[:, r * 128:(r + 1) * 128], tri4[:, 0:128]), inc=True)
    t_hm = dma(lini2, hmat[:], hmat_in)
    wt("scalar", t_tri)
    tk = dma(lini, x1r[0:1, :], bo_in)
    wt("vector", tk)
    t_bobf = op("vector", lambda e: e.tensor_copy(xnst[0:1, :], x1r[0:1, :]),
                inc=True)
    wt("tensor", t_ones)
    wt("tensor", t_bobf)
    t_BO = None
    for j in range(4):
        bk = pb[7]
        bk.gw(p, "tensor")
        tmm = op("tensor", lambda e, j=j, b_=bk: e.matmul(
            b_.ap[:, :], onesb[0:1, :], xnst[0:1, j * 512:(j + 1) * 512],
            start=True, stop=True), inc=True)
        bk.wr = tmm
        bk.gr(p, "scalar")
        t_BO = op("scalar", lambda e, j=j, b_=bk: e.copy(
            BO[:, j * 512:(j + 1) * 512], b_.ap[:, :]), inc=True)
        bk.rds.append(t_BO)
    U["xnst"] = t_BO
    U["x1r"] = t_BO

    def eoff(hh, slot):
        return (hh * 6 + slot) * 512
    stair = []
    for hh in range(2):
        for j in range(1, 4):
            stair.append(op("vector", lambda e, hh=hh, j=j: e.memset(
                ebuf[:, eoff(hh, j):eoff(hh, j) + 128 * j], 0.0), inc=True))

    cast_tok = {}
    w13tok = [None] * 32

    def cast_steps(name):
        steps = []
        for s in range(4):
            def f(s=s, name=name):
                cast_tok[name] = dma(
                    lcast[name],
                    w_bf[name][s].rearrange("q (i c) -> q i c", c=512),
                    w_in[name].rearrange("(i q) (s c) -> s q i c",
                                         q=128, c=512)[s])
            steps.append(f)
        return steps

    prep = (cast_steps("wk") + cast_steps("wv") + cast_steps("wq")
            + cast_steps("wo"))
    prep_i = {"prep": 0}

    def drain(n):
        k = 0
        while prep_i["prep"] < len(prep) and k < n:
            prep[prep_i["prep"]]()
            prep_i["prep"] += 1
            k += 1

    def prep_gen():
        for s in range(32):
            for w, name in ((0, "w1"), (1, "w3")):
                w13tok[s] = dma(
                    lcast["w13"],
                    w13img[s][:, w * 4096:(w + 1) * 4096].rearrange(
                        "q (i c) -> q i c", c=256),
                    w_in[name].rearrange("(i q) (s c) -> s q i c",
                                         q=128, c=256)[s])
            yield 0
        p1_ld = [None, None]
        p1_st = [None, None]
        for q in range(128):
            lpr = q % 2
            wt("gpsimd", p1_ld[lpr])
            wt("gpsimd", FU["zld"][lpr])
            ldk = dma(lfld[lpr], zld[lpr][0:64, :],
                      w_in["w2"].rearrange("(t r) d -> t r d", r=128)[:, q, :])
            last_mm = None
            for ds in range(4):
                bk = pb[6 + (ds % 2)]
                bk.gw(p, "tensor")
                wt("tensor", ldk)
                wt("tensor", t_hm)
                tmm = op("tensor", lambda e, b_=bk, ds=ds, lpr=lpr: e.matmul(
                    b_.ap[0:64, :], hmat[0:64, 0:64],
                    zld[lpr][0:64, ds * 512:(ds + 1) * 512],
                    start=True, stop=True), inc=True)
                bk.wr = tmm
                last_mm = tmm
                sidx = ds % 2
                eng = "scalar" if sidx == 0 else "vector"
                wt(eng, p1_st[sidx])
                wt(eng, FU["w2b"][sidx])
                bk.gr(p, eng)
                if eng == "scalar":
                    trd = op("scalar", lambda e, s=sidx, b_=bk: e.copy(
                        w2b[s][0:64, 0:512], b_.ap[0:64, :]), inc=True)
                else:
                    trd = op("vector", lambda e, s=sidx, b_=bk: e.tensor_copy(
                        w2b[s][0:64, 0:512], b_.ap[0:64, :]), inc=True)
                bk.rds.append(trd)
                wt("sync", trd)
                stt = dma(lfs[sidx], w2f64[ds, q], w2b[sidx][0:64, 0:512])
                p1_st[sidx] = stt
            p1_ld[lpr] = last_mm
            FU["zld"][lpr] = last_mm
            FU["w2b"][0] = p1_st[0]
            FU["w2b"][1] = p1_st[1]
            yield 4
        p1_mark = [(s, s.n) for s in lfs]
        for j in range(64):
            jpr = j % 2
            wt("scalar", FU["zld"][jpr])
            wt("scalar", p1_mark)
            ldk = dma(lzld[jpr], zld[jpr][:].rearrange("q (ds c) -> q ds c",
                                                      c=512),
                      w2f64[:, :, j, :].rearrange("ds q c -> q ds c"))
            ev = None
            for ds in range(4):
                bk = pb[4 + (ds % 2)]
                bk.gw(p, "tensor")
                wt("tensor", ldk)
                wt("tensor", t_hm)
                tmm = op("tensor", lambda e, b_=bk, ds=ds, jpr=jpr: e.matmul(
                    b_.ap[:, :], hmat[:, :],
                    zld[jpr][:, ds * 512:(ds + 1) * 512],
                    start=True, stop=True), inc=True)
                bk.wr = tmm
                FU["zld"][jpr] = tmm
                bk.gr(p, "scalar")
                wt("scalar", FU["w2b"][jpr])
                ev = op("scalar", lambda e, b_=bk, ds=ds, jpr=jpr: e.copy(
                    w2b[jpr][:, ds * 512:(ds + 1) * 512], b_.ap[:, :]),
                    inc=True)
                bk.rds.append(ev)
            wt("sync", ev)
            stt = dma(lw2h, w2h[:, j, :, :].rearrange("ds q c -> q ds c"),
                      w2b[jpr][:].rearrange("q (ds c) -> q ds c", c=512))
            FU["w2b"][jpr] = stt
            yield 4
        FU["fold_mark"] = [(lw2h, lw2h.n)]

    def layer_norm(src_ap, dst_ap, parity, in_tok):
        so = 32 * parity
        wt("vector", in_tok)
        wt("vector", U["stat"][parity])
        tprev = None
        for q in range(4):
            tprev = op("vector", lambda e, q=q, so=so, s_=src_ap: e.bn_stats(
                statb[:, so + 6 * q:so + 6 * q + 6],
                s_[:, q * 512:(q + 1) * 512]), inc=(q == 3))
        wt("vector", tprev)
        t_ag = op("vector", lambda e, so=so: e.bn_aggr(
            statb[:, so + 24:so + 26], statb[:, so + 0:so + 24]), inc=True)
        wt("scalar", t_ag)
        t_ln = op("scalar", lambda e, so=so: e.activation(
            statb[:, so + 26:so + 27], statb[:, so + 25:so + 26], AF.Ln,
            bias=1e-5), inc=True)
        wt("scalar", t_ln)
        t_rs = op("scalar", lambda e, so=so: e.activation(
            statb[:, so + 27:so + 28], statb[:, so + 26:so + 27], AF.Exp,
            scale=-0.5), inc=True)
        wt("vector", t_rs)
        t_mr = op("vector", lambda e, so=so: e.tensor_tensor(
            statb[:, so + 28:so + 29], statb[:, so + 24:so + 25],
            statb[:, so + 27:so + 28], ALU.mult), inc=True)
        wt("vector", t_mr)
        t_xn = op("vector", lambda e, so=so, s_=src_ap, d_=dst_ap:
                  e.tensor_scalar(d_, s_, statb[:, so + 27:so + 28],
                                  statb[:, so + 28:so + 29], ALU.mult,
                                  ALU.subtract), inc=True)
        U["stat"][parity] = t_xn
        return t_xn

    xn1_done = {}

    def phase_ln1(b):
        toks = []
        for i in range(NTT):
            wt("scalar", U["x1r"])
            tk = dma(lx1r, x1r[:], x_in[b, i * 128:(i + 1) * 128, :])
            wt("vector", U["xnst"])
            t_xn = layer_norm(x1r[:], xnst[:], i % 2, tk)
            U["x1r"] = t_xn
            wt("gpsimd", t_xn)
            last = dma(lstx, xn1_s[b, i * 128:(i + 1) * 128, :], xnst[:])
            U["xnst"] = last
            toks.append(last)
        xn1_done[b] = [toks[-1]]

    def load_xnt(bi, src, b, c, pre):
        wt("sync", U["xnt"][bi])
        wt("sync", pre)
        tok = None
        for i in range(16):
            tok = dmat(lts[bi], xnt[bi][:, i * 512:(i + 1) * 512],
                       src[b, c * 512:(c + 1) * 512, i * 128:(i + 1) * 128])
        return tok

    def load_slab(name, s):
        wt("scalar", U["slab"])
        wt("scalar", cast_tok[name])
        return dma(lslab, slab[:].rearrange("q (i c) -> q i c", c=512),
                   w_bf[name][s].rearrange("q (i c) -> q i c", c=512))

    def rope(ppr, col0, t_ev):
        rt = prow[ppr]
        wt("vector", t_ev)
        wt("vector", t_cos)
        wt("vector", t_sin)
        t4 = None
        for (d0, s0) in ((0, 32), (32, 0), (64, 96), (96, 64)):
            t4 = op("vector", lambda e, d0=d0, s0=s0, rt=rt, col0=col0:
                    e.tensor_tensor(ropec[d0:d0 + 32, :], rt[s0:s0 + 32, :],
                                    sintb[s0:s0 + 32, col0:col0 + QC],
                                    ALU.mult), inc=(d0 == 96))
        t_qc = op("vector", lambda e, rt=rt, col0=col0: e.tensor_tensor(
            rt[:], rt[:], costb[:, col0:col0 + QC], ALU.mult), inc=True)
        wt("vector", t4)
        wt("vector", t_qc)
        return op("vector", lambda e, rt=rt: e.tensor_tensor(
            rt[:], rt[:], ropec[:], ALU.add), inc=True)

    def phase_kv(b):
        for c in range(NCH):
            cpr = c % 2
            xk = load_xnt(cpr, xn1_s, b, c, xn1_done[b])
            for s in range(4):
                sl = load_slab("wk", s)
                for J in range(4 * s, 4 * s + 4):
                    bk = pb[J % 2]
                    bk.gw(p, "tensor")
                    wt("tensor", xk)
                    wt("tensor", sl)
                    tmm = None
                    for i in range(16):
                        tmm = op("tensor", lambda e, b_=bk, i=i, J=J, s=s,
                                 cpr=cpr: e.matmul(
                            b_.ap[:, :],
                            slab[:, i * 512 + (J - 4 * s) * 128:
                                 i * 512 + (J - 4 * s) * 128 + 128],
                            xnt[cpr][:, i * 512:(i + 1) * 512],
                            start=(i == 0), stop=(i == 15)), inc=(i == 15))
                    bk.wr = tmm
                    U["slab"] = tmm
                    U["xnt"][cpr] = tmm
                    ppr = J % 2
                    bk.gr(p, "scalar")
                    wt("scalar", U["prow"][ppr])
                    t_ev = op("scalar", lambda e, b_=bk, ppr=ppr: e.copy(
                        prow[ppr][:], b_.ap[:, :]), inc=True)
                    bk.rds.append(t_ev)
                    t_rp = rope(ppr, c * QC, t_ev)
                    wt("gpsimd", t_rp)
                    k_last = dma(lktl[ppr], kT_s[b, J * 128:(J + 1) * 128,
                                                 c * QC:(c + 1) * QC],
                                 prow[ppr][:])
                    U["prow"][ppr] = k_last
            for s in range(4):
                sl = load_slab("wv", s)
                for tt in range(4):
                    bk = pb[2 + tt % 2]
                    bk.gw(p, "tensor")
                    wt("tensor", xk)
                    wt("tensor", sl)
                    tmm = None
                    for i in range(16):
                        tmm = op("tensor", lambda e, b_=bk, i=i, tt=tt,
                                 cpr=cpr: e.matmul(
                            b_.ap[:, :],
                            xnt[cpr][:, i * 512 + tt * 128:
                                      i * 512 + tt * 128 + 128],
                            slab[:, i * 512:(i + 1) * 512],
                            start=(i == 0), stop=(i == 15)), inc=(i == 15))
                    bk.wr = tmm
                    U["slab"] = tmm
                    U["xnt"][cpr] = tmm
                    vpr = tt % 2
                    bk.gr(p, "scalar")
                    wt("scalar", U["vstg"][vpr])
                    t_ev = op("scalar", lambda e, b_=bk, vpr=vpr: e.copy(
                        zst[vpr][:], b_.ap[:, :]), inc=True)
                    bk.rds.append(t_ev)
                    wt("gpsimd", t_ev)
                    v_last = dma(lval[vpr], va_s[b, (c * 4 + tt) * 128:
                                                 (c * 4 + tt) * 128 + 128,
                                                 s * 512:(s + 1) * 512],
                                 zst[vpr][:])
                    U["vstg"][vpr] = v_last
                    FU["zst"][vpr] = v_last

    kv_mark = None

    def emit_kv_loads(b, c, hp):
        q0 = c * QC
        hpr = hp % 2
        glob = (hp == 15)
        kmin = 0 if (glob or c < 2) else 512
        kend = q0 + 512
        nkt = (kend - kmin) // 128
        wt("gpsimd", AU["kbuf"][hpr])
        wt("gpsimd", kv_mark)
        kb = dma(lkb[hpr], kbuf[hpr][:, 0:kend - kmin],
                 kT_s[b, hp * 128:(hp + 1) * 128, kmin:kend])
        wt("gpsimd", AU["vbuf"][hpr])
        vb = None
        for m in range(2):
            vb = dma(lvb[hpr], vbuf[hpr][:, 0:nkt * 130]
                     .rearrange("q (n h c) -> q n h c", h=2, c=65)
                     [:, :, m, 0:64],
                     va_s[b, kmin:kend,
                          hp * 128 + m * 64:hp * 128 + m * 64 + 64]
                     .rearrange("(n q) c -> q n c", q=128))
        wt("vector", vb)
        t_vo = op("vector", lambda e, hpr=hpr, nkt=nkt: e.memset(
            vbuf[hpr][:, 0:nkt * 130]
            .rearrange("q (n h c) -> q n h c", h=2, c=65)[:, :, :, 64:65],
            1.0), inc=True)
        AU["kb"][hpr] = kb
        AU["vb"][hpr] = vb
        AU["vo"][hpr] = t_vo

    def score_av(b, c, hp, qpr):
        q0 = c * QC
        hpr = hp % 2
        glob = (hp == 15)
        kmin = 0 if (glob or c < 2) else 512
        kend = q0 + 512
        full_k0 = list(range(kmin, q0, 128))
        stra_k0 = [q0 + 128 * j for j in range(4)]
        kb = AU["kb"][hpr]
        vb = AU["vb"][hpr]
        t_vo = AU["vo"][hpr]
        qt_tok = U["qt"][qpr]

        exp_tok = {}
        last_mm = [None, None]
        for j, k0 in enumerate(stra_k0):
            for hh in range(2):
                bk = pb[hh]
                bk.gw(p, "tensor")
                wt("tensor", kb)
                wt("tensor", qt_tok)
                tmm = op("tensor", lambda e, b_=bk, hh=hh, k0=k0, hpr=hpr,
                         kmin=kmin, qpr=qpr: e.matmul(
                    b_.ap[:, :],
                    kbuf[hpr][hh * 64:hh * 64 + 64,
                              k0 - kmin:k0 - kmin + 128],
                    QT[qpr][hh * 64:hh * 64 + 64, :],
                    start=True, stop=True, tile_position=(hh * 64, 0)),
                    inc=True)
                bk.wr = tmm
                last_mm[hh] = tmm
                bk.gr(p, "scalar")
                wt("scalar", AU["ebuf"].get((hh, j)))
                t_ex = op("scalar", lambda e, b_=bk, hh=hh, j=j:
                          e.activation(
                    ebuf[:, eoff(hh, j) + 128 * j:eoff(hh, j) + 512],
                    b_.ap[:, 128 * j:512], AF.Exp, scale=SCL, bias=-SHIFT),
                    inc=True)
                bk.rds.append(t_ex)
                exp_tok[(hh, j)] = t_ex
            yield 2
        band = []
        for hh in range(2):
            for j in range(4):
                wt("vector", exp_tok[(hh, j)])
            wt("vector", t_tri)
            for tz in stair:
                wt("vector", tz)
            band.append(op("vector", lambda e, hh=hh: e.tensor_tensor(
                ebuf[:, eoff(hh, 0):eoff(hh, 0) + 4 * 640]
                .rearrange("q (j c) -> q j c", c=640)[:, :, 0:128],
                ebuf[:, eoff(hh, 0):eoff(hh, 0) + 4 * 640]
                .rearrange("q (j c) -> q j c", c=640)[:, :, 0:128],
                tri4[:].rearrange("q (j c) -> q j c", c=128), ALU.mult),
                inc=True))
        yield 0

        seq = [("f", k0) for k0 in full_k0] + [("s", k0) for k0 in stra_k0]
        o_tok = [None, None]
        for idx, (kind, k0) in enumerate(seq):
            first = (idx == 0)
            lastt = (idx == len(seq) - 1)
            if kind == "f":
                slot = 4 + (idx % 2)
                for hh in range(2):
                    bk = pb[hh]
                    bk.gw(p, "tensor")
                    wt("tensor", kb)
                    wt("tensor", qt_tok)
                    tmm = op("tensor", lambda e, b_=bk, hh=hh, k0=k0,
                             hpr=hpr, kmin=kmin, qpr=qpr: e.matmul(
                        b_.ap[:, :],
                        kbuf[hpr][hh * 64:hh * 64 + 64,
                                  k0 - kmin:k0 - kmin + 128],
                        QT[qpr][hh * 64:hh * 64 + 64, :],
                        start=True, stop=True, tile_position=(hh * 64, 0)),
                        inc=True)
                    bk.wr = tmm
                    last_mm[hh] = tmm
                    bk.gr(p, "scalar")
                    wt("scalar", AU["ebuf"].get((hh, slot)))
                    t_ex = op("scalar", lambda e, b_=bk, hh=hh, slot=slot:
                              e.activation(
                        ebuf[:, eoff(hh, slot):eoff(hh, slot) + 512],
                        b_.ap[:, :], AF.Exp, scale=SCL, bias=-SHIFT),
                        inc=True)
                    bk.rds.append(t_ex)
                    exp_tok[(hh, slot)] = t_ex
            else:
                slot = (k0 - q0) // 128
            for hh in range(2):
                obk = pb[2 + hh]
                if first:
                    obk.gw(p, "tensor")
                if kind == "f":
                    wt("tensor", exp_tok[(hh, slot)])
                else:
                    wt("tensor", band[hh])
                wt("tensor", vb)
                wt("tensor", t_vo)
                kti = (k0 - kmin) // 128
                tmo = op("tensor", lambda e, b_=obk, hh=hh, kti=kti,
                         hpr=hpr, slot=slot, first=first, lastt=lastt:
                         e.matmul(
                    b_.ap[0:65, :],
                    vbuf[hpr][:, kti * 130 + hh * 65:
                              kti * 130 + hh * 65 + 65],
                    ebuf[:, eoff(hh, slot):eoff(hh, slot) + 512],
                    start=first, stop=lastt), inc=True)
                o_tok[hh] = tmo
                if kind == "f":
                    AU["ebuf"][(hh, slot)] = tmo
            yield 4 if kind == "f" else 2
        pb[2].wr = o_tok[0]
        pb[3].wr = o_tok[1]
        AU["kbuf"][hpr] = o_tok[1]
        AU["vbuf"][hpr] = o_tok[1]
        for j in range(4):
            AU["ebuf"][(0, j)] = o_tok[0]
            AU["ebuf"][(1, j)] = o_tok[1]
        U["qt"][qpr] = o_tok[1]

        t_ot = None
        for hh in range(2):
            bk = pb[2 + hh]
            bk.gr(p, "vector")
            wt("vector", AU["rbf"][hh])

            def _rcp(e, b_=bk, hh=hh):
                with nc.allow_low_precision(
                        reason="1/n row in bf16 feeds a bf16 broadcast "
                               "matmul; 2^-9 uniform scale error"):
                    return e.reciprocal(rbf[hh][:], b_.ap[64:65, :])
            t_rc = op("vector", _rcp, inc=True)
            bk.gr(p, "scalar")
            wt("scalar", AU["oub"][hh])
            t_ou = op("scalar", lambda e, b_=bk, hh=hh: e.copy(
                oub[hh][:], b_.ap[0:64, :]), inc=True)
            bk.rds.append(t_rc)
            bk.rds.append(t_ou)
            rbk = pb[hh]
            rbk.gw(p, "tensor")
            wt("tensor", t_rc)
            wt("tensor", t_ones)
            t_rb = op("tensor", lambda e, b_=rbk, hh=hh: e.matmul(
                b_.ap[0:64, :], onesb[0:1, 0:64], rbf[hh][:],
                start=True, stop=True), inc=True)
            rbk.wr = t_rb
            AU["rbf"][hh] = t_rb
            rbk.gr(p, "vector")
            wt("vector", t_ou)
            t_ot = op("vector", lambda e, b_=rbk, hh=hh, hp=hp:
                      e.tensor_tensor(
                OT[hh * 64:hh * 64 + 64, hp * 512:(hp + 1) * 512],
                oub[hh][:], b_.ap[0:64, :], ALU.mult), inc=True)
            rbk.rds.append(t_ot)
            AU["oub"][hh] = t_ot
        yield 2
        ot_tok[hp] = t_ot

    ot_tok = {}
    x1tok = {}
    xn2tok = {}

    def a_gen(b, c):
        xk = load_xnt(0, xn1_s, b, c, xn1_done[b])
        yield 1
        for s in range(4):
            sl = load_slab("wq", s)
            for J in range(4 * s, 4 * s + 4):
                hp = J
                emit_kv_loads(b, c, hp)
                qpr = J % 2
                bk = pb[qpr]
                bk.gw(p, "tensor")
                wt("tensor", xk)
                wt("tensor", sl)
                tmm = None
                for i in range(16):
                    tmm = op("tensor", lambda e, b_=bk, i=i, J=J, s=s:
                             e.matmul(
                        b_.ap[:, :],
                        slab[:, i * 512 + (J - 4 * s) * 128:
                             i * 512 + (J - 4 * s) * 128 + 128],
                        xnt[0][:, i * 512:(i + 1) * 512],
                        start=(i == 0), stop=(i == 15)), inc=(i == 15))
                bk.wr = tmm
                U["slab"] = tmm
                U["xnt"][0] = tmm
                bk.gr(p, "scalar")
                wt("scalar", U["prow"][qpr])
                t_ev = op("scalar", lambda e, b_=bk, qpr=qpr: e.copy(
                    prow[qpr][:], b_.ap[:, :]), inc=True)
                bk.rds.append(t_ev)
                t_rp = rope(qpr, c * QC, t_ev)
                wt("vector", t_rp)
                wt("vector", U["qt"][qpr])
                t_cp = op("vector", lambda e, qpr=qpr: e.tensor_copy(
                    QT[qpr][:], prow[qpr][:]), inc=True)
                U["prow"][qpr] = t_cp
                U["qt"][qpr] = t_cp
                yield 16
                yield from score_av(b, c, hp, qpr)
        for s in range(4):
            sl = load_slab("wo", s)
            for tau in range(4):
                trow = c * 512 + tau * 128
                bk = pb[tau % 2]
                bk.gw(p, "tensor")
                wt("tensor", sl)
                tmm = None
                for i in range(16):
                    wt("tensor", ot_tok[i])
                    tmm = op("tensor", lambda e, b_=bk, i=i, tau=tau:
                             e.matmul(
                        b_.ap[:, :],
                        OT[:, i * 512 + tau * 128:i * 512 + tau * 128 + 128],
                        slab[:, i * 512:(i + 1) * 512],
                        start=(i == 0), stop=(i == 15)), inc=(i == 15))
                bk.wr = tmm
                U["slab"] = tmm
                xpr = tau % 2
                wt("scalar", U["xb"][xpr])
                tkx = dma(lxb[xpr], xb[xpr][:],
                          x_in[b, trow:trow + 128, s * 512:(s + 1) * 512])
                bk.gr(p, "vector")
                wt("vector", tkx)
                wt("vector", t_BO)
                wt("vector", U["x1p"][xpr])
                t_a1 = op("vector", lambda e, b_=bk, s=s, xpr=xpr:
                          e.tensor_tensor(
                    x1p[xpr][:], b_.ap[:, :], BO[:, s * 512:(s + 1) * 512],
                    ALU.add), inc=True)
                wt("vector", t_a1)
                t_a2 = op("vector", lambda e, xpr=xpr: e.tensor_tensor(
                    x1p[xpr][:], x1p[xpr][:], xb[xpr][:], ALU.add),
                    inc=True)
                bk.rds.append(t_a2)
                U["xb"][xpr] = t_a2
                wt("gpsimd", t_a2)
                stx = dma(lx1s[xpr],
                          x1_s[b, trow:trow + 128, s * 512:(s + 1) * 512],
                          x1p[xpr][:])
                U["x1p"][xpr] = stx
                x1tok[(b, c, s, tau)] = stx
                yield 16
        toks = []
        for tau in range(4):
            trow = c * 512 + tau * 128
            wt("scalar", [x1tok[(b, c, s, tau)] for s in range(4)])
            wt("scalar", U["x1r"])
            tk = dma(lx1r, x1r[:], x1_s[b, trow:trow + 128, :])
            wt("vector", U["xnst"])
            t_xn = layer_norm(x1r[:], xnst[:], tau % 2, tk)
            U["x1r"] = t_xn
            wt("gpsimd", t_xn)
            st = dma(lsty, xn2_s[b, trow:trow + 128, :], xnst[:])
            U["xnst"] = st
            toks.append(st)
            yield 1
        xn2tok[(b, c)] = toks

    def f_gen(b, c, zp):
        wt("sync", FU["xnt2"])
        wt("sync", xn2tok[(b, c)])
        xk = None
        for i in range(16):
            xk = dmat(lts[1], xnt[1][:, i * 512:(i + 1) * 512],
                      xn2_s[b, c * 512:(c + 1) * 512,
                            i * 128:(i + 1) * 128])
        yield 1
        ztok = [None] * 64
        wt("gpsimd", FU["zuser"][zp])
        up_last = None
        for s in range(32):
            spr = s % 2
            wt("scalar", FU["fsl"][spr])
            wt("scalar", w13tok[s])
            tw = dma(lfsl[spr], fsl[spr][:], w13img[s])
            for Jl in range(2):
                J = 2 * s + Jl
                bkA = pb[4 + 2 * (J % 2)]
                bkB = pb[5 + 2 * (J % 2)]
                for bk, woff in ((bkA, 0), (bkB, 4096)):
                    bk.gw(p, "tensor")
                    wt("tensor", xk)
                    wt("tensor", tw)
                    tmm = None
                    for i in range(16):
                        tmm = op("tensor", lambda e, b_=bk, i=i, Jl=Jl,
                                 spr=spr, woff=woff: e.matmul(
                            b_.ap[:, :],
                            fsl[spr][:, woff + i * 256 + Jl * 128:
                                     woff + i * 256 + Jl * 128 + 128],
                            xnt[1][:, i * 512:(i + 1) * 512],
                            start=(i == 0), stop=(i == 15)),
                            inc=(i == 15))
                    bk.wr = tmm
                    FU["fsl"][spr] = tmm
                    FU["xnt2"] = tmm
                    up_last = tmm
                gpr = J % 2
                bkA.gr(p, "scalar")
                wt("scalar", FU["gbuf"][gpr])
                t_si = op("scalar", lambda e, b_=bkA, gpr=gpr:
                          e.activation(gbuf[gpr][:], b_.ap[:, :], AF.Silu),
                          inc=True)
                bkA.rds.append(t_si)
                bkB.gr(p, "vector")
                wt("vector", t_si)
                wt("vector", FU["zst"][gpr])
                t_g = op("vector", lambda e, b_=bkB, gpr=gpr:
                         e.scalar_tensor_tensor(
                    zst[gpr][:], b_.ap[:, :], GSC, gbuf[gpr][:],
                    ALU.mult, ALU.mult), inc=True)
                bkB.rds.append(t_g)
                FU["gbuf"][gpr] = t_g
                wt("gpsimd", t_g)
                stz = dma(lz[gpr], z_s[zp, J], zst[gpr][:])
                FU["zst"][gpr] = stz
                ztok[J] = stz
                yield 32
        zl_last = None
        for ds in range(4):
            for tau in range(4):
                pb[4 + tau].gw(p, "tensor")
            for Jg in range(16):
                gpr = Jg % 2
                wt("scalar", FU["w2b"][gpr])
                wt("scalar", FU["fold_mark"])
                tw2 = dma(lw2b[gpr],
                          w2b[gpr][:].rearrange("q (j c) -> q j c", c=512),
                          w2h[ds, 4 * Jg:4 * Jg + 4].rearrange(
                              "j q c -> q j c"))
                wt("scalar", FU["zld"][gpr])
                wt("scalar", [ztok[4 * Jg + k] for k in range(4)])
                tzl = dma(lzld[gpr],
                          zld[gpr][:].rearrange("q (j c) -> q j c", c=512),
                          z_s[zp, 4 * Jg:4 * Jg + 4].rearrange(
                              "j q c -> q j c"))
                zl_last = tzl
                last4 = [None] * 4
                for J4 in range(4):
                    J = 4 * Jg + J4
                    for tau in range(4):
                        bk = pb[4 + tau]
                        wt("tensor", tw2)
                        wt("tensor", tzl)
                        tmm = op("tensor", lambda e, b_=bk, J4=J4, tau=tau,
                                 gpr=gpr, ds=ds, Jg=Jg: e.matmul(
                            b_.ap[:, :],
                            zld[gpr][:, J4 * 512 + tau * 128:
                                     J4 * 512 + tau * 128 + 128],
                            w2b[gpr][:, J4 * 512:(J4 + 1) * 512],
                            start=(Jg == 0 and J4 == 0),
                            stop=(Jg == 15 and J4 == 3)),
                            inc=(J4 == 3))
                        last4[tau] = tmm
                FU["w2b"][gpr] = last4[3]
                FU["zld"][gpr] = last4[3]
                yield 16
            for tau in range(4):
                bk = pb[4 + tau]
                bk.wr = last4[tau]
                trow = c * 512 + tau * 128
                xpr = tau % 2
                wt("scalar", FU["x1b"][xpr])
                wt("scalar", x1tok[(b, c, ds, tau)])
                tk1 = dma(lx1b[xpr], x1b[xpr][:],
                          x1_s[b, trow:trow + 128,
                               ds * 512:(ds + 1) * 512])
                bk.gr(p, "vector")
                wt("vector", tk1)
                wt("vector", FU["x2b"][xpr])
                t_x2 = op("vector", lambda e, b_=bk, xpr=xpr:
                          e.tensor_tensor(
                    x2b[xpr][:], b_.ap[:, :], x1b[xpr][:], ALU.add),
                    inc=True)
                bk.rds.append(t_x2)
                FU["x1b"][xpr] = t_x2
                wt("gpsimd", t_x2)
                ost = dma(lout[xpr], out_t[b, trow:trow + 128,
                                           ds * 512:(ds + 1) * 512],
                          x2b[xpr][:])
                FU["x2b"][xpr] = ost
                yield 1
        FU["zuser"][zp] = zl_last

    drain(8)
    phase_ln1(0)
    drain(8)
    phase_kv(0)
    phase_ln1(1)
    phase_kv(1)
    kv_mark = [(s, s.n) for s in lktl + lval]
    barrier()

    def interleave(gens):
        live = [[g, float(w), 0.0] for (g, w) in gens]
        while live:
            g = min(live, key=lambda st: st[2] / st[1])
            try:
                inc = next(g[0])
                g[2] += max(inc, 0.25)
            except StopIteration:
                live.remove(g)

    def run_all(g):
        for _ in g:
            pass

    chunks = [(b, c) for b in range(B) for c in range(NCH)]
    score_w = {0: 128, 1: 256, 2: 264, 3: 392}
    for i, (b, c) in enumerate(chunks):
        aw = 256 + 256 + 2 * score_w[c] + 64
        if i == 0:
            interleave([(a_gen(b, c), aw), (prep_gen(), 850)])
        else:
            pb_, pc = chunks[i - 1]
            interleave([(a_gen(b, c), aw),
                        (f_gen(pb_, pc, (i - 1) % 2), 3200)])
    bl, cl = chunks[-1]
    run_all(f_gen(bl, cl, (len(chunks) - 1) % 2))

    for _s in sems.values():
        p.wait("sync", (_s, _s.n))

    with nc.Block() as blk:
        p.run(blk)
    es.close()
    return nc


def _tables():
    inv = 10000.0 ** (-(np.arange(0, HD, 2, dtype=np.float64) / HD))
    pos = np.arange(T, dtype=np.float64)
    fr = pos[None, :] * inv[:, None]
    cos32 = np.cos(fr).astype(np.float32)
    sin32 = np.sin(fr).astype(np.float32)
    cost = np.zeros((128, T), dtype=np.float32)
    sint = np.zeros((128, T), dtype=np.float32)
    for half in range(2):
        b0 = half * 64
        cost[b0:b0 + 32] = cos32
        cost[b0 + 32:b0 + 64] = cos32
        sint[b0:b0 + 32] = sin32
        sint[b0 + 32:b0 + 64] = -sin32
    qq = np.arange(128)[None, :]
    kk = np.arange(128)[:, None]
    tri = (qq >= kk).astype(np.float32).astype(BFD)
    pc = np.zeros((128, 128), dtype=np.int64)
    for i in range(128):
        for j in range(128):
            pc[i, j] = bin(i & j).count("1")
    hmat = np.where(pc % 2 == 0, 1.0, -1.0).astype(np.float32).astype(BFD)
    return cost, sint, tri, hmat


_NC_CACHE = None


def kernel(L0_attn_hidden_states, L0_attn_Wq, L0_attn_Wk, L0_attn_Wv, L0_attn_Wo,
           L0_attn_bo, L0_ffn_W1, L0_ffn_W2, L0_ffn_W3):
    global _NC_CACHE
    x = np.ascontiguousarray(L0_attn_hidden_states, dtype=np.float32)
    cost, sint, tri, hmat = _tables()
    if _NC_CACHE is None:
        _NC_CACHE = build_nc()
    nc = _NC_CACHE
    base = {
        "wq": np.ascontiguousarray(np.asarray(L0_attn_Wq, dtype=np.float32).T),
        "wk": np.ascontiguousarray(np.asarray(L0_attn_Wk, dtype=np.float32).T),
        "wv": np.ascontiguousarray(np.asarray(L0_attn_Wv, dtype=np.float32).T),
        "wo": np.ascontiguousarray(np.asarray(L0_attn_Wo, dtype=np.float32).T),
        "bo": np.ascontiguousarray(L0_attn_bo, dtype=np.float32).reshape(1, D),
        "w1": np.ascontiguousarray(np.asarray(L0_ffn_W1, dtype=np.float32).T),
        "w2": np.ascontiguousarray(np.asarray(L0_ffn_W2, dtype=np.float32).T),
        "w3": np.ascontiguousarray(np.asarray(L0_ffn_W3, dtype=np.float32).T),
        "cost_t": cost, "sint_t": sint,
        "tri_t": np.asarray(tri), "hmat_t": np.asarray(hmat),
    }
    in_maps = []
    for core in range(8):
        m = dict(base)
        m["x"] = x[core * 2:core * 2 + 2]
        in_maps.append(m)
    res = run_bass_kernel_spmd(nc, in_maps, core_ids=list(range(8)))
    return np.concatenate([r["out"] for r in res.results], axis=0)


# version 10
# speedup vs baseline: 1.0306x; 1.0306x over previous
import numpy as np
import ml_dtypes
import concourse.bass as bass
import concourse.mybir as mybir
from concourse.bass_utils import run_bass_kernel_spmd
from contextlib import ExitStack

F32 = mybir.dt.float32
BF16 = mybir.dt.bfloat16
AF = mybir.ActivationFunctionType
ALU = mybir.AluOpType
BFD = ml_dtypes.bfloat16

B, T, D, NH, HD, HID = 2, 2048, 2048, 32, 64, 8192
NTT, NCH, QC = 16, 4, 512
SHIFT = 8.0
SCL = 0.125
GSC = float(1.0 / np.sqrt(HID))

ENGS = ("sync", "tensor", "vector", "scalar", "gpsimd")


class Sem:
    __slots__ = ("h", "n", "key", "eng")

    def __init__(self, h, key):
        self.h, self.n, self.key = h, 0, key
        self.eng = None

    def bump(self, k):
        self.n += k
        return self.n


class Plan:
    def __init__(self):
        self.ops = {e: [] for e in ENGS}
        self.hwm = {}

    def emit(self, eng, fn):
        self.ops[eng].append(fn)

    def wait(self, eng, tok):
        if tok is None:
            return
        sem, val = tok
        if val <= 0:
            return
        k = (eng, sem.key)
        if self.hwm.get(k, -1) >= val:
            return
        self.hwm[k] = val
        self.emit(eng, lambda e, s=sem.h, v=val: e.wait_ge(s, v))

    def run(self, blk):
        for eng in ENGS:
            ops_ = self.ops[eng]

            def mk(o):
                def body(e):
                    for f in o:
                        f(e)
                return body
            getattr(blk, eng)(mk(ops_))


class Bank:
    def __init__(self, ap):
        self.ap = ap
        self.wr = None
        self.rds = []

    def gw(self, p, eng):
        for t in self.rds:
            p.wait(eng, t)
        self.rds = []

    def gr(self, p, eng):
        p.wait(eng, self.wr)


def build_nc():
    nc = bass.Bass("TRN2", target_bir_lowering=False)
    p = Plan()
    es = ExitStack()

    def din(name, shape, dt=F32):
        return nc.dram_tensor(name, shape, dt, kind="ExternalInput").ap()

    x_in = din("x", [B, T, D])
    w_in = {n: din(n, [D, D]) for n in ("wq", "wk", "wv", "wo")}
    bo_in = din("bo", [1, D])
    w_in["w1"] = din("w1", [D, HID])
    w_in["w2"] = din("w2", [HID, D])
    w_in["w3"] = din("w3", [D, HID])
    cost_in = din("cost_t", [128, T])
    sint_in = din("sint_t", [128, T])
    tri_in = din("tri_t", [128, 128], BF16)
    hmat_in = din("hmat_t", [128, 128], BF16)
    out_t = nc.dram_tensor("out", [B, T, D], F32, kind="ExternalOutput").ap()

    def scr(name, shape, dt=BF16):
        return nc.dram_tensor(name, shape, dt, kind="Internal").ap()

    w_bf = {n: scr(n + "_bf", [4, 128, 16 * 512])
            for n in ("wq", "wk", "wv", "wo")}
    w13img = scr("w13img", [32, 128, 2 * 16 * 256])
    w2f64 = scr("w2f64", [4, 128, 64, 512])
    w2h = scr("w2h", [4, 64, 128, 512])
    xn1_s = scr("xn1_s", [B, T, D])
    xn2_s = scr("xn2_s", [B, T, D])
    kT_s = scr("kT_s", [B, D, T])
    va_s = scr("va_s", [B, T, D])
    x1_s = scr("x1_s", [B, T, D], F32)
    z_s = scr("z_s", [2, 64, 128, 512])

    sems = {}
    for n in ("spe", "ssc", "sdv", "sgp"):
        sems[n] = Sem(es.enter_context(nc.semaphore(n)), n)

    def lane(name, eng):
        s = Sem(es.enter_context(nc.semaphore(name)), name)
        s.eng = eng
        sems[name] = s
        return s

    lts = [lane("lts0", "sync"), lane("lts1", "sync")]
    lslab = lane("lslab", "sync")
    lfsl = [lane("lfsl0", "sync"), lane("lfsl1", "sync")]
    lotl = lane("lotl", "sync")
    lw2b = [lane("lw2b0", "sync"), lane("lw2b1", "sync")]
    lzld = [lane("lzld0", "sync"), lane("lzld1", "sync")]
    lfs = [lane("lfs0", "sync"), lane("lfs1", "sync")]
    lw2h = lane("lw2h", "sync")
    lini = lane("lini", "gpsimd")
    lini2 = lane("lini2", "gpsimd")
    lxb = [lane("lxb0", "gpsimd"), lane("lxb1", "gpsimd")]
    lx1b = [lane("lx1b0", "gpsimd"), lane("lx1b1", "gpsimd")]
    lx1r = lane("lx1r", "gpsimd")
    lstx = lane("lstx", "gpsimd")
    lsty = lane("lsty", "gpsimd")
    lktl = [lane("lktl0", "gpsimd"), lane("lktl1", "gpsimd")]
    lval = [lane("lval0", "gpsimd"), lane("lval1", "gpsimd")]
    lx1s = [lane("lx1s0", "gpsimd"), lane("lx1s1", "gpsimd")]
    lout = [lane("lout0", "gpsimd"), lane("lout1", "gpsimd")]
    lkb = [lane("lkb0", "gpsimd"), lane("lkb1", "gpsimd")]
    lvb = [lane("lvb0", "gpsimd"), lane("lvb1", "gpsimd")]
    lz = [lane("lz0", "gpsimd"), lane("lz1", "gpsimd")]
    lcast = {n: lane("lc_" + n, "gpsimd")
             for n in ("wk", "wv", "wq", "wo", "w13")}
    lfld = [lane(f"lfld{i}", "gpsimd") for i in range(4)]
    spe, ssc, sdv = sems["spe"], sems["ssc"], sems["sdv"]
    ESEM = {"tensor": spe, "scalar": ssc, "vector": sdv, "gpsimd": sems["sgp"]}

    def dma(q, out_ap, in_ap, gates=()):
        for g in gates:
            p.wait(q.eng, g)
        v = q.bump(16)
        p.emit(q.eng, lambda e, o=out_ap, i=in_ap, sh=q.h:
               e.dma_start(o, i).then_inc(sh, 16))
        return (q, v)

    def dmat(q, out_ap, in_ap):
        v = q.bump(16)
        p.emit(q.eng, lambda e, o=out_ap, i=in_ap, sh=q.h:
               e.dma_start_transpose(o, i).then_inc(sh, 16))
        return (q, v)

    def op(eng, fn, inc=False):
        if inc:
            s = ESEM[eng]
            v = s.bump(1)
            p.emit(eng, lambda e, f=fn, sh=s.h: f(e).then_inc(sh, 1))
            return (s, v)
        p.emit(eng, fn)
        return None

    def wt(eng, tok):
        if isinstance(tok, list):
            for t in tok:
                p.wait(eng, t)
        else:
            p.wait(eng, tok)

    def barrier():
        toks = [(s, s.n) for s in sems.values()]
        for eng in ENGS:
            for t in toks:
                p.wait(eng, t)

    def sb(name, shape, dt=F32):
        return es.enter_context(nc.sbuf_tensor(name, shape, dt))

    cpool = sb("cpool", [128, 4])
    hmat = sb("hmat", [128, 128], BF16)
    onesb = sb("onesb", [1, 128], BF16)
    costb = sb("costb", [128, T], BF16)
    sintb = sb("sintb", [128, T], BF16)
    tri4 = sb("tri4", [128, 512], BF16)
    BO = sb("BO", [128, D])
    statb = sb("statb", [128, 64])
    xnt = [sb("xnt0", [128, 16 * 512], BF16), sb("xnt1", [128, 16 * 512],
                                                 BF16)]
    slab = sb("slab", [128, 16 * 512], BF16)
    QT = [sb(f"QT{i}", [128, 512], BF16) for i in range(2)]
    prow = [sb(f"prow{i}", [128, 512], BF16) for i in range(2)]
    ropec = sb("ropec", [128, 512], BF16)
    OT = sb("OT", [128, 16 * 512], BF16)
    ebuf = sb("ebuf", [128, 2 * 6 * 512], BF16)
    kbuf = [sb(f"kbuf{i}", [128, 2048], BF16) for i in range(2)]
    vbuf = [sb(f"vbuf{i}", [128, 16 * 130], BF16) for i in range(2)]
    o65 = [sb(f"o65_{i}", [128, 1024], BF16) for i in range(2)]
    rbf = [sb(f"rbf{i}", [1, 512], BF16) for i in range(2)]
    x1p = [sb(f"x1p{i}", [128, 512], F32) for i in range(2)]
    x1r = sb("x1r", [128, D], F32)
    xb = [sb(f"xb{i}", [128, 512], F32) for i in range(2)]
    xnst = sb("xnst", [128, D], BF16)
    fsl = [sb(f"fsl{i}", [128, 16 * 512], BF16) for i in range(2)]
    gbuf = [sb(f"gbuf{i}", [128, 512], BF16) for i in range(2)]
    zst = [sb(f"zst{i}", [128, 512], BF16) for i in range(2)]
    w2b = [sb(f"w2b{i}", [128, 4 * 512], BF16) for i in range(2)]
    zld = [sb(f"zld{i}", [128, 4 * 512], BF16) for i in range(2)]
    x1b = [sb(f"x1b{i}", [128, 512], F32) for i in range(2)]
    x2b = [sb(f"x2b{i}", [128, 512], F32) for i in range(2)]

    pb = [Bank(es.enter_context(nc.psum_tensor(f"pb{i}", [128, 512], F32)))
          for i in range(8)]

    U = {"xnt": [None, None], "slab": None, "xb": [None, None],
         "xnst": None, "prow": [None, None], "qt": [None, None],
         "stat": [None, None], "vstg": [None, None], "x1p": [None, None],
         "x1r": None, "kslabB": None, "vslabA": None, "vslabB": None}
    FU = {"fsl": [None, None], "gbuf": [None, None], "zst": [None, None],
          "w2b": [None, None], "zld": [None, None], "x1b": [None, None],
          "x2b": [None, None], "xnt2": None, "zuser": [None, None]}
    AU = {"kbuf": [None, None], "vbuf": [None, None], "kb": [None, None],
          "vb": [None, None], "vo": [None, None],
          "ebuf": {}, "o65": [None, None], "e65": [None, None],
          "rbf": [None, None]}

    op("vector", lambda e: e.memset(cpool[:, 0:1], -SHIFT))
    op("vector", lambda e: e.memset(cpool[:, 1:2], 1e-5))
    t_ones = op("vector", lambda e: e.memset(onesb[0:1, :], 1.0), inc=True)
    nc.const_aps.register(F32, -SHIFT, cpool[:, 0:1])
    nc.const_aps.register(F32, 1e-5, cpool[:, 1:2])

    tk = dma(lini, x1r[:], cost_in)
    wt("vector", tk)
    t_cos = op("vector", lambda e: e.tensor_copy(costb[:], x1r[:]), inc=True)
    wt("gpsimd", t_cos)
    tk = dma(lini, x1r[:], sint_in)
    wt("vector", tk)
    t_sin = op("vector", lambda e: e.tensor_copy(sintb[:], x1r[:]), inc=True)
    wt("gpsimd", t_sin)
    tk = dma(lini, tri4[:, 0:128], tri_in)
    wt("vector", tk)
    t_tri = None
    for r in range(1, 4):
        wt("vector", t_tri)
        t_tri = op("vector", lambda e, r=r: e.tensor_copy(
            tri4[:, r * 128:(r + 1) * 128], tri4[:, 0:128]), inc=True)
    t_hm = dma(lini2, hmat[:], hmat_in)
    wt("gpsimd", t_tri)
    tk = dma(lini, x1r[0:1, :], bo_in)
    wt("vector", tk)
    t_bobf = op("vector", lambda e: e.tensor_copy(xnst[0:1, :], x1r[0:1, :]),
                inc=True)
    wt("tensor", t_ones)
    wt("tensor", t_bobf)
    t_BO = None
    for j in range(4):
        bk = pb[7]
        bk.gw(p, "tensor")
        tmm = op("tensor", lambda e, j=j, b_=bk: e.matmul(
            b_.ap[:, :], onesb[0:1, :], xnst[0:1, j * 512:(j + 1) * 512],
            start=True, stop=True), inc=True)
        bk.wr = tmm
        bk.gr(p, "scalar")
        t_BO = op("scalar", lambda e, j=j, b_=bk: e.copy(
            BO[:, j * 512:(j + 1) * 512], b_.ap[:, :]), inc=True)
        bk.rds.append(t_BO)
    U["xnst"] = t_BO
    U["x1r"] = t_BO

    def eoff(hh, slot):
        return (hh * 6 + slot) * 512
    stair = []
    for hh in range(2):
        for j in range(1, 4):
            stair.append(op("vector", lambda e, hh=hh, j=j: e.memset(
                ebuf[:, eoff(hh, j):eoff(hh, j) + 128 * j], 0.0), inc=True))

    cast_tok = {}
    w13tok = [None] * 32

    def cast_steps(name):
        steps = []
        for s in range(4):
            def f(s=s, name=name):
                cast_tok[name] = dma(
                    lcast[name],
                    w_bf[name][s].rearrange("q (i c) -> q i c", c=512),
                    w_in[name].rearrange("(i q) (s c) -> s q i c",
                                         q=128, c=512)[s])
            steps.append(f)
        return steps

    prep = (cast_steps("wk") + cast_steps("wv") + cast_steps("wq")
            + cast_steps("wo"))
    prep_i = {"prep": 0}

    def drain(n):
        k = 0
        while prep_i["prep"] < len(prep) and k < n:
            prep[prep_i["prep"]]()
            prep_i["prep"] += 1
            k += 1

    def prep1_gen():
        p1_ld = [None] * 4
        p1_st = [None, None]
        p1_user = [None, None]
        for q in range(128):
            sl4 = q % 4
            zb, zr = zld[sl4 // 2], (sl4 % 2) * 64
            ldk = dma(lfld[sl4], zb[zr:zr + 64, :],
                      w_in["w2"].rearrange("(t r) d -> t r d", r=128)[:, q, :],
                      gates=[p1_ld[sl4]])
            last_mm = None
            for ds in range(4):
                bk = pb[6 + (ds % 2)]
                bk.gw(p, "tensor")
                wt("tensor", ldk)
                wt("tensor", t_hm)
                tmm = op("tensor", lambda e, b_=bk, ds=ds, zb=zb, zr=zr:
                         e.matmul(
                    b_.ap[0:64, :], hmat[zr:zr + 64, 0:64],
                    zb[zr:zr + 64, ds * 512:(ds + 1) * 512],
                    start=True, stop=True, tile_position=(zr, 0)),
                    inc=True)
                bk.wr = tmm
                last_mm = tmm
                sidx = ds % 2
                eng = "scalar" if sidx == 0 else "vector"
                wt(eng, p1_user[sidx])
                bk.gr(p, eng)
                if eng == "scalar":
                    trd = op("scalar", lambda e, s=sidx, b_=bk: e.copy(
                        w2b[s][0:64, 0:512], b_.ap[0:64, :]), inc=True)
                else:
                    trd = op("vector", lambda e, s=sidx, b_=bk: e.tensor_copy(
                        w2b[s][0:64, 0:512], b_.ap[0:64, :]), inc=True)
                bk.rds.append(trd)
                stt = dma(lfs[sidx], w2f64[ds, q], w2b[sidx][0:64, 0:512],
                          gates=[trd, p1_st[sidx]])
                p1_st[sidx] = stt
                p1_user[sidx] = stt
            p1_ld[sl4] = last_mm
            FU["zld"][sl4 // 2] = last_mm
            FU["w2b"][0] = p1_st[0]
            FU["w2b"][1] = p1_st[1]
            yield 4
        FU["p1_mark"] = [(s, s.n) for s in lfs]

    def prep2_gen():
        for s in range(32):
            for w, name in ((0, "w1"), (1, "w3")):
                w13tok[s] = dma(
                    lcast["w13"],
                    w13img[s][:, w * 4096:(w + 1) * 4096].rearrange(
                        "q (i c) -> q i c", c=256),
                    w_in[name].rearrange("(i q) (s c) -> s q i c",
                                         q=128, c=256)[s])
            yield 0
        for j in range(64):
            jpr = j % 2
            ldk = dma(lzld[jpr],
                      zld[jpr][:].rearrange("q (ds c) -> q ds c", c=512),
                      w2f64[:, :, j, :].rearrange("ds q c -> q ds c"),
                      gates=[FU["zld"][jpr]] + FU["p1_mark"])
            ev = None
            for ds in range(4):
                bk = pb[4 + (ds % 2)]
                bk.gw(p, "tensor")
                wt("tensor", ldk)
                wt("tensor", t_hm)
                tmm = op("tensor", lambda e, b_=bk, ds=ds, jpr=jpr: e.matmul(
                    b_.ap[:, :], hmat[:, :],
                    zld[jpr][:, ds * 512:(ds + 1) * 512],
                    start=True, stop=True), inc=True)
                bk.wr = tmm
                FU["zld"][jpr] = tmm
                bk.gr(p, "scalar")
                wt("scalar", FU["w2b"][jpr])
                ev = op("scalar", lambda e, b_=bk, ds=ds, jpr=jpr: e.copy(
                    w2b[jpr][:, ds * 512:(ds + 1) * 512], b_.ap[:, :]),
                    inc=True)
                bk.rds.append(ev)
            stt = dma(lw2h, w2h[:, j, :, :].rearrange("ds q c -> q ds c"),
                      w2b[jpr][:].rearrange("q (ds c) -> q ds c", c=512),
                      gates=[ev])
            FU["w2b"][jpr] = stt
            yield 4
        FU["fold_mark"] = [(lw2h, lw2h.n)]

    def layer_norm(src_ap, dst_ap, parity, in_tok):
        so = 32 * parity
        wt("vector", in_tok)
        wt("vector", U["stat"][parity])
        tprev = None
        for q in range(4):
            tprev = op("vector", lambda e, q=q, so=so, s_=src_ap: e.bn_stats(
                statb[:, so + 6 * q:so + 6 * q + 6],
                s_[:, q * 512:(q + 1) * 512]), inc=(q == 3))
        wt("vector", tprev)
        t_ag = op("vector", lambda e, so=so: e.bn_aggr(
            statb[:, so + 24:so + 26], statb[:, so + 0:so + 24]), inc=True)
        wt("scalar", t_ag)
        t_ln = op("scalar", lambda e, so=so: e.activation(
            statb[:, so + 26:so + 27], statb[:, so + 25:so + 26], AF.Ln,
            bias=1e-5), inc=True)
        wt("scalar", t_ln)
        t_rs = op("scalar", lambda e, so=so: e.activation(
            statb[:, so + 27:so + 28], statb[:, so + 26:so + 27], AF.Exp,
            scale=-0.5), inc=True)
        wt("vector", t_rs)
        t_mr = op("vector", lambda e, so=so: e.tensor_tensor(
            statb[:, so + 28:so + 29], statb[:, so + 24:so + 25],
            statb[:, so + 27:so + 28], ALU.mult), inc=True)
        wt("vector", t_mr)
        t_xn = op("vector", lambda e, so=so, s_=src_ap, d_=dst_ap:
                  e.tensor_scalar(d_, s_, statb[:, so + 27:so + 28],
                                  statb[:, so + 28:so + 29], ALU.mult,
                                  ALU.subtract), inc=True)
        U["stat"][parity] = t_xn
        return t_xn

    xn1_done = {0: None, 1: None}

    def ln1_tile(b, i):
        tk = dma(lx1r, x1r[:], x_in[b, i * 128:(i + 1) * 128, :],
                 gates=[U["x1r"]])
        wt("vector", U["xnst"])
        t_xn = layer_norm(x1r[:], xnst[:], i % 2, tk)
        U["x1r"] = t_xn
        wt("gpsimd", t_xn)
        last = dma(lstx, xn1_s[b, i * 128:(i + 1) * 128, :], xnst[:])
        U["xnst"] = last
        if i == NTT - 1:
            xn1_done[b] = [last]

    def load_xnt(bi, lane_, src, b, c, pre):
        wt(lane_.eng, U["xnt"][bi])
        wt(lane_.eng, pre)
        tok = None
        for i in range(16):
            tok = dmat(lane_, xnt[bi][:, i * 512:(i + 1) * 512],
                       src[b, c * 512:(c + 1) * 512, i * 128:(i + 1) * 128])
        return tok

    def load_slab(name, s):
        return dma(lslab, slab[:], w_bf[name][s],
                   gates=[U["slab"], cast_tok[name]])

    def rope(ppr, col0, t_ev):
        rt = prow[ppr]
        wt("vector", t_ev)
        wt("vector", t_cos)
        wt("vector", t_sin)
        t4 = None
        for (d0, s0) in ((0, 32), (32, 0), (64, 96), (96, 64)):
            t4 = op("vector", lambda e, d0=d0, s0=s0, rt=rt, col0=col0:
                    e.tensor_tensor(ropec[d0:d0 + 32, :], rt[s0:s0 + 32, :],
                                    sintb[s0:s0 + 32, col0:col0 + QC],
                                    ALU.mult), inc=(d0 == 96))
        t_qc = op("vector", lambda e, rt=rt, col0=col0: e.tensor_tensor(
            rt[:], rt[:], costb[:, col0:col0 + QC], ALU.mult), inc=True)
        wt("vector", t4)
        wt("vector", t_qc)
        return op("vector", lambda e, rt=rt: e.tensor_tensor(
            rt[:], rt[:], ropec[:], ALU.add), inc=True)

    def kv_chunk(b, c):
        cpr = c % 2
        xk = load_xnt(cpr, lts[cpr], xn1_s, b, c, xn1_done[b])
        for s in range(4):
            if s % 2 == 0:
                kbuf_w, klane = slab, lslab
                sl = dma(lslab, slab[:], w_bf["wk"][s],
                         gates=[U["slab"], cast_tok["wk"]])
                U_key = "slab"
            else:
                kbuf_w, klane = fsl[1], lfsl[1]
                sl = dma(lfsl[1], fsl[1][:], w_bf["wk"][s],
                         gates=[U["kslabB"], cast_tok["wk"]])
                U_key = "kslabB"
            for J in range(4 * s, 4 * s + 4):
                bk = pb[J % 2]
                bk.gw(p, "tensor")
                wt("tensor", xk)
                wt("tensor", sl)
                tmm = None
                for i in range(16):
                    tmm = op("tensor", lambda e, b_=bk, i=i, J=J, s=s,
                             cpr=cpr, w_=kbuf_w: e.matmul(
                        b_.ap[:, :],
                        w_[:, i * 512 + (J - 4 * s) * 128:
                           i * 512 + (J - 4 * s) * 128 + 128],
                        xnt[cpr][:, i * 512:(i + 1) * 512],
                        start=(i == 0), stop=(i == 15)), inc=(i == 15))
                bk.wr = tmm
                U[U_key] = tmm
                U["xnt"][cpr] = tmm
                ppr = J % 2
                bk.gr(p, "scalar")
                wt("scalar", U["prow"][ppr])
                t_ev = op("scalar", lambda e, b_=bk, ppr=ppr: e.copy(
                    prow[ppr][:], b_.ap[:, :]), inc=True)
                bk.rds.append(t_ev)
                t_rp = rope(ppr, c * QC, t_ev)
                wt("gpsimd", t_rp)
                k_last = dma(lktl[ppr], kT_s[b, J * 128:(J + 1) * 128,
                                             c * QC:(c + 1) * QC],
                             prow[ppr][:])
                U["prow"][ppr] = k_last
                yield 16
        for s in range(4):
            if s % 2 == 0:
                vbuf_w = fsl[0]
                sl = dma(lfsl[0], fsl[0][:], w_bf["wv"][s],
                         gates=[U["vslabA"], cast_tok["wv"]])
                U_key = "vslabA"
            else:
                vbuf_w = OT
                sl = dma(lotl, OT[:], w_bf["wv"][s],
                         gates=[U["vslabB"], cast_tok["wv"]])
                U_key = "vslabB"
            for tt in range(4):
                bk = pb[2 + tt % 2]
                bk.gw(p, "tensor")
                wt("tensor", xk)
                wt("tensor", sl)
                tmm = None
                for i in range(16):
                    tmm = op("tensor", lambda e, b_=bk, i=i, tt=tt,
                             cpr=cpr, w_=vbuf_w: e.matmul(
                        b_.ap[:, :],
                        xnt[cpr][:, i * 512 + tt * 128:
                                  i * 512 + tt * 128 + 128],
                        w_[:, i * 512:(i + 1) * 512],
                        start=(i == 0), stop=(i == 15)), inc=(i == 15))
                bk.wr = tmm
                U[U_key] = tmm
                U["xnt"][cpr] = tmm
                vpr = tt % 2
                bk.gr(p, "scalar")
                wt("scalar", U["vstg"][vpr])
                t_ev = op("scalar", lambda e, b_=bk, vpr=vpr: e.copy(
                    zst[vpr][:], b_.ap[:, :]), inc=True)
                bk.rds.append(t_ev)
                wt("gpsimd", t_ev)
                v_last = dma(lval[vpr], va_s[b, (c * 4 + tt) * 128:
                                             (c * 4 + tt) * 128 + 128,
                                             s * 512:(s + 1) * 512],
                             zst[vpr][:])
                U["vstg"][vpr] = v_last
                FU["zst"][vpr] = v_last
                yield 16

    def phaseA_gen():
        for i in range(NTT):
            ln1_tile(0, i)
            yield 0.5
        for c in range(NCH):
            yield from kv_chunk(0, c)
            for i in range(4):
                ln1_tile(1, 4 * c + i)
                yield 0.5
        for c in range(NCH):
            yield from kv_chunk(1, c)

    kv_mark = None

    def emit_kv_loads(b, c, hp):
        q0 = c * QC
        hpr = hp % 2
        glob = (hp == 15)
        kmin = 0 if (glob or c < 2) else 512
        kend = q0 + 512
        nkt = (kend - kmin) // 128
        kb = dma(lkb[hpr], kbuf[hpr][:, 0:kend - kmin],
                 kT_s[b, hp * 128:(hp + 1) * 128, kmin:kend],
                 gates=[AU["kbuf"][hpr]] + kv_mark)
        wt("gpsimd", AU["vbuf"][hpr])
        vb = None
        for m in range(2):
            vb = dma(lvb[hpr], vbuf[hpr][:, 0:nkt * 130]
                     .rearrange("q (n h c) -> q n h c", h=2, c=65)
                     [:, :, m, 0:64],
                     va_s[b, kmin:kend,
                          hp * 128 + m * 64:hp * 128 + m * 64 + 64]
                     .rearrange("(n q) c -> q n c", q=128))
        wt("vector", vb)
        t_vo = op("vector", lambda e, hpr=hpr, nkt=nkt: e.memset(
            vbuf[hpr][:, 0:nkt * 130]
            .rearrange("q (n h c) -> q n h c", h=2, c=65)[:, :, :, 64:65],
            1.0), inc=True)
        AU["kb"][hpr] = kb
        AU["vb"][hpr] = vb
        AU["vo"][hpr] = t_vo

    ot_tok = {}
    x1tok = {}
    xn2tok = {}

    def emit_norm(hp, hpr):
        t_ot = None
        wt("vector", AU["e65"][hpr])
        for hh in range(2):
            col0 = hh * 512
            wt("vector", AU["rbf"][hh])

            def _rcp(e, hh=hh, hpr=hpr, col0=col0):
                with nc.allow_low_precision(
                        reason="1/n row in bf16 feeds a bf16 broadcast "
                               "matmul; 2^-9 uniform scale error"):
                    return e.reciprocal(rbf[hh][:],
                                        o65[hpr][64:65, col0:col0 + 512])
            t_rc = op("vector", _rcp, inc=True)
            rbk = pb[hh]
            rbk.gw(p, "tensor")
            wt("tensor", t_rc)
            wt("tensor", t_ones)
            t_rb = op("tensor", lambda e, b_=rbk, hh=hh: e.matmul(
                b_.ap[0:64, :], onesb[0:1, 0:64], rbf[hh][:],
                start=True, stop=True), inc=True)
            rbk.wr = t_rb
            AU["rbf"][hh] = t_rb
            rbk.gr(p, "vector")
            t_ot = op("vector", lambda e, b_=rbk, hh=hh, hp=hp, hpr=hpr,
                      col0=col0: e.tensor_tensor(
                OT[hh * 64:hh * 64 + 64, hp * 512:(hp + 1) * 512],
                o65[hpr][0:64, col0:col0 + 512], b_.ap[0:64, :], ALU.mult),
                inc=True)
            rbk.rds.append(t_ot)
        AU["o65"][hpr] = t_ot
        ot_tok[hp] = t_ot

    def score_av(b, c, hp, qpr):
        q0 = c * QC
        hpr = hp % 2
        glob = (hp == 15)
        kmin = 0 if (glob or c < 2) else 512
        kend = q0 + 512
        full_k0 = list(range(kmin, q0, 128))
        stra_k0 = [q0 + 128 * j for j in range(4)]
        kb = AU["kb"][hpr]
        vb = AU["vb"][hpr]
        t_vo = AU["vo"][hpr]
        qt_tok = U["qt"][qpr]

        exp_tok = {}
        for j, k0 in enumerate(stra_k0):
            for hh in range(2):
                bk = pb[hh]
                bk.gw(p, "tensor")
                wt("tensor", kb)
                wt("tensor", qt_tok)
                tmm = op("tensor", lambda e, b_=bk, hh=hh, k0=k0, hpr=hpr,
                         kmin=kmin, qpr=qpr: e.matmul(
                    b_.ap[:, :],
                    kbuf[hpr][hh * 64:hh * 64 + 64,
                              k0 - kmin:k0 - kmin + 128],
                    QT[qpr][hh * 64:hh * 64 + 64, :],
                    start=True, stop=True, tile_position=(hh * 64, 0)),
                    inc=True)
                bk.wr = tmm
                bk.gr(p, "scalar")
                wt("scalar", AU["ebuf"].get((hh, j)))
                t_ex = op("scalar", lambda e, b_=bk, hh=hh, j=j:
                          e.activation(
                    ebuf[:, eoff(hh, j) + 128 * j:eoff(hh, j) + 512],
                    b_.ap[:, 128 * j:512], AF.Exp, scale=SCL, bias=-SHIFT),
                    inc=True)
                bk.rds.append(t_ex)
                exp_tok[(hh, j)] = t_ex
            yield 2
        band = []
        for hh in range(2):
            for j in range(4):
                wt("vector", exp_tok[(hh, j)])
            wt("vector", t_tri)
            for tz in stair:
                wt("vector", tz)
            band.append(op("vector", lambda e, hh=hh: e.tensor_tensor(
                ebuf[:, eoff(hh, 0):eoff(hh, 0) + 4 * 640]
                .rearrange("q (j c) -> q j c", c=640)[:, :, 0:128],
                ebuf[:, eoff(hh, 0):eoff(hh, 0) + 4 * 640]
                .rearrange("q (j c) -> q j c", c=640)[:, :, 0:128],
                tri4[:].rearrange("q (j c) -> q j c", c=128), ALU.mult),
                inc=True))
        yield 0

        seq = [("f", k0) for k0 in full_k0] + [("s", k0) for k0 in stra_k0]
        o_tok = [None, None]
        for idx, (kind, k0) in enumerate(seq):
            first = (idx == 0)
            lastt = (idx == len(seq) - 1)
            if kind == "f":
                slot = 4 + (idx % 2)
                for hh in range(2):
                    bk = pb[hh]
                    bk.gw(p, "tensor")
                    wt("tensor", kb)
                    wt("tensor", qt_tok)
                    tmm = op("tensor", lambda e, b_=bk, hh=hh, k0=k0,
                             hpr=hpr, kmin=kmin, qpr=qpr: e.matmul(
                        b_.ap[:, :],
                        kbuf[hpr][hh * 64:hh * 64 + 64,
                                  k0 - kmin:k0 - kmin + 128],
                        QT[qpr][hh * 64:hh * 64 + 64, :],
                        start=True, stop=True, tile_position=(hh * 64, 0)),
                        inc=True)
                    bk.wr = tmm
                    bk.gr(p, "scalar")
                    wt("scalar", AU["ebuf"].get((hh, slot)))
                    t_ex = op("scalar", lambda e, b_=bk, hh=hh, slot=slot:
                              e.activation(
                        ebuf[:, eoff(hh, slot):eoff(hh, slot) + 512],
                        b_.ap[:, :], AF.Exp, scale=SCL, bias=-SHIFT),
                        inc=True)
                    bk.rds.append(t_ex)
                    exp_tok[(hh, slot)] = t_ex
            else:
                slot = (k0 - q0) // 128
            for hh in range(2):
                obk = pb[2 + hh]
                if first:
                    obk.gw(p, "tensor")
                if kind == "f":
                    wt("tensor", exp_tok[(hh, slot)])
                else:
                    wt("tensor", band[hh])
                wt("tensor", vb)
                wt("tensor", t_vo)
                kti = (k0 - kmin) // 128
                tmo = op("tensor", lambda e, b_=obk, hh=hh, kti=kti,
                         hpr=hpr, slot=slot, first=first, lastt=lastt:
                         e.matmul(
                    b_.ap[0:65, :],
                    vbuf[hpr][:, kti * 130 + hh * 65:
                              kti * 130 + hh * 65 + 65],
                    ebuf[:, eoff(hh, slot):eoff(hh, slot) + 512],
                    start=first, stop=lastt), inc=True)
                o_tok[hh] = tmo
                if kind == "f":
                    AU["ebuf"][(hh, slot)] = tmo
            yield 4 if kind == "f" else 2
        pb[2].wr = o_tok[0]
        pb[3].wr = o_tok[1]
        AU["kbuf"][hpr] = o_tok[1]
        AU["vbuf"][hpr] = o_tok[1]
        for j in range(4):
            AU["ebuf"][(0, j)] = o_tok[0]
            AU["ebuf"][(1, j)] = o_tok[1]
        U["qt"][qpr] = o_tok[1]

        wt("scalar", AU["o65"][hpr])
        t_e65 = None
        for hh in range(2):
            bk = pb[2 + hh]
            bk.gr(p, "scalar")
            t_e65 = op("scalar", lambda e, b_=bk, hh=hh, hpr=hpr: e.copy(
                o65[hpr][0:65, hh * 512:hh * 512 + 512], b_.ap[0:65, :]),
                inc=True)
            bk.rds.append(t_e65)
        AU["e65"][hpr] = t_e65
        yield 1

    def a_gen(b, c):
        xk = load_xnt(0, lts[0], xn1_s, b, c, xn1_done[b])
        yield 1
        for s in range(4):
            sl = load_slab("wq", s)
            for J in range(4 * s, 4 * s + 4):
                hp = J
                emit_kv_loads(b, c, hp)
                qpr = J % 2
                bk = pb[qpr]
                bk.gw(p, "tensor")
                wt("tensor", xk)
                wt("tensor", sl)
                tmm = None
                for i in range(16):
                    tmm = op("tensor", lambda e, b_=bk, i=i, J=J, s=s:
                             e.matmul(
                        b_.ap[:, :],
                        slab[:, i * 512 + (J - 4 * s) * 128:
                             i * 512 + (J - 4 * s) * 128 + 128],
                        xnt[0][:, i * 512:(i + 1) * 512],
                        start=(i == 0), stop=(i == 15)), inc=(i == 15))
                bk.wr = tmm
                U["slab"] = tmm
                U["xnt"][0] = tmm
                bk.gr(p, "scalar")
                wt("scalar", U["prow"][qpr])
                t_ev = op("scalar", lambda e, b_=bk, qpr=qpr: e.copy(
                    prow[qpr][:], b_.ap[:, :]), inc=True)
                bk.rds.append(t_ev)
                t_rp = rope(qpr, c * QC, t_ev)
                wt("vector", t_rp)
                wt("vector", U["qt"][qpr])
                t_cp = op("vector", lambda e, qpr=qpr: e.tensor_copy(
                    QT[qpr][:], prow[qpr][:]), inc=True)
                U["prow"][qpr] = t_cp
                U["qt"][qpr] = t_cp
                if hp > 0:
                    emit_norm(hp - 1, (hp - 1) % 2)
                yield 18
                yield from score_av(b, c, hp, qpr)
        emit_norm(15, 1)
        yield 2
        for s in range(4):
            sl = load_slab("wo", s)
            for tau in range(4):
                trow = c * 512 + tau * 128
                bk = pb[tau % 2]
                bk.gw(p, "tensor")
                wt("tensor", sl)
                tmm = None
                for i in range(16):
                    wt("tensor", ot_tok[i])
                    tmm = op("tensor", lambda e, b_=bk, i=i, tau=tau:
                             e.matmul(
                        b_.ap[:, :],
                        OT[:, i * 512 + tau * 128:i * 512 + tau * 128 + 128],
                        slab[:, i * 512:(i + 1) * 512],
                        start=(i == 0), stop=(i == 15)), inc=(i == 15))
                bk.wr = tmm
                U["slab"] = tmm
                xpr = tau % 2
                tkx = dma(lxb[xpr], xb[xpr][:],
                          x_in[b, trow:trow + 128, s * 512:(s + 1) * 512],
                          gates=[U["xb"][xpr]])
                bk.gr(p, "vector")
                wt("vector", tkx)
                wt("vector", t_BO)
                wt("vector", U["x1p"][xpr])
                t_a1 = op("vector", lambda e, b_=bk, s=s, xpr=xpr:
                          e.tensor_tensor(
                    x1p[xpr][:], b_.ap[:, :], BO[:, s * 512:(s + 1) * 512],
                    ALU.add), inc=True)
                wt("vector", t_a1)
                t_a2 = op("vector", lambda e, xpr=xpr: e.tensor_tensor(
                    x1p[xpr][:], x1p[xpr][:], xb[xpr][:], ALU.add),
                    inc=True)
                bk.rds.append(t_a2)
                U["xb"][xpr] = t_a2
                wt("gpsimd", t_a2)
                stx = dma(lx1s[xpr],
                          x1_s[b, trow:trow + 128, s * 512:(s + 1) * 512],
                          x1p[xpr][:])
                U["x1p"][xpr] = stx
                x1tok[(b, c, s, tau)] = stx
                yield 16
        toks = []
        for tau in range(4):
            trow = c * 512 + tau * 128
            tk = dma(lx1r, x1r[:], x1_s[b, trow:trow + 128, :],
                     gates=[U["x1r"]] + [x1tok[(b, c, s, tau)]
                                         for s in range(4)])
            wt("vector", U["xnst"])
            t_xn = layer_norm(x1r[:], xnst[:], tau % 2, tk)
            U["x1r"] = t_xn
            wt("gpsimd", t_xn)
            st = dma(lsty, xn2_s[b, trow:trow + 128, :], xnst[:])
            U["xnst"] = st
            toks.append(st)
            yield 1
        xn2tok[(b, c)] = toks

    def f_gen(b, c, zp):
        wt("sync", FU["xnt2"])
        wt("sync", xn2tok[(b, c)])
        xk = None
        for i in range(16):
            xk = dmat(lts[1], xnt[1][:, i * 512:(i + 1) * 512],
                      xn2_s[b, c * 512:(c + 1) * 512,
                            i * 128:(i + 1) * 128])
        yield 1
        ztok = [None] * 64
        wt("gpsimd", FU["zuser"][zp])
        for s in range(32):
            spr = s % 2
            tw = dma(lfsl[spr], fsl[spr][:], w13img[s],
                     gates=[FU["fsl"][spr], w13tok[s]])
            for Jl in range(2):
                J = 2 * s + Jl
                bkA = pb[4 + 2 * (J % 2)]
                bkB = pb[5 + 2 * (J % 2)]
                for bk, woff in ((bkA, 0), (bkB, 4096)):
                    bk.gw(p, "tensor")
                    wt("tensor", xk)
                    wt("tensor", tw)
                    tmm = None
                    for i in range(16):
                        tmm = op("tensor", lambda e, b_=bk, i=i, Jl=Jl,
                                 spr=spr, woff=woff: e.matmul(
                            b_.ap[:, :],
                            fsl[spr][:, woff + i * 256 + Jl * 128:
                                     woff + i * 256 + Jl * 128 + 128],
                            xnt[1][:, i * 512:(i + 1) * 512],
                            start=(i == 0), stop=(i == 15)),
                            inc=(i == 15))
                    bk.wr = tmm
                    FU["fsl"][spr] = tmm
                    FU["xnt2"] = tmm
                gpr = J % 2
                bkA.gr(p, "scalar")
                wt("scalar", FU["gbuf"][gpr])
                t_si = op("scalar", lambda e, b_=bkA, gpr=gpr:
                          e.activation(gbuf[gpr][:], b_.ap[:, :], AF.Silu),
                          inc=True)
                bkA.rds.append(t_si)
                bkB.gr(p, "vector")
                wt("vector", t_si)
                wt("vector", FU["zst"][gpr])
                t_g = op("vector", lambda e, b_=bkB, gpr=gpr:
                         e.scalar_tensor_tensor(
                    zst[gpr][:], b_.ap[:, :], GSC, gbuf[gpr][:],
                    ALU.mult, ALU.mult), inc=True)
                bkB.rds.append(t_g)
                FU["gbuf"][gpr] = t_g
                wt("gpsimd", t_g)
                stz = dma(lz[gpr], z_s[zp, J], zst[gpr][:])
                FU["zst"][gpr] = stz
                ztok[J] = stz
                yield 32
        zl_last = None
        for ds in range(4):
            for tau in range(4):
                pb[4 + tau].gw(p, "tensor")
            for Jg in range(16):
                gpr = Jg % 2
                tw2 = dma(lw2b[gpr],
                          w2b[gpr][:].rearrange("q (j c) -> q j c", c=512),
                          w2h[ds, 4 * Jg:4 * Jg + 4].rearrange(
                              "j q c -> q j c"),
                          gates=[FU["w2b"][gpr]] + FU["fold_mark"])
                tzl = dma(lzld[gpr],
                          zld[gpr][:].rearrange("q (j c) -> q j c", c=512),
                          z_s[zp, 4 * Jg:4 * Jg + 4].rearrange(
                              "j q c -> q j c"),
                          gates=[FU["zld"][gpr]]
                          + [ztok[4 * Jg + k] for k in range(4)])
                zl_last = tzl
                last4 = [None] * 4
                for J4 in range(4):
                    for tau in range(4):
                        bk = pb[4 + tau]
                        wt("tensor", tw2)
                        wt("tensor", tzl)
                        tmm = op("tensor", lambda e, b_=bk, J4=J4, tau=tau,
                                 gpr=gpr, Jg=Jg: e.matmul(
                            b_.ap[:, :],
                            zld[gpr][:, J4 * 512 + tau * 128:
                                     J4 * 512 + tau * 128 + 128],
                            w2b[gpr][:, J4 * 512:(J4 + 1) * 512],
                            start=(Jg == 0 and J4 == 0),
                            stop=(Jg == 15 and J4 == 3)),
                            inc=(J4 == 3))
                        last4[tau] = tmm
                FU["w2b"][gpr] = last4[3]
                FU["zld"][gpr] = last4[3]
                yield 16
            for tau in range(4):
                bk = pb[4 + tau]
                bk.wr = last4[tau]
                trow = c * 512 + tau * 128
                xpr = tau % 2
                tk1 = dma(lx1b[xpr], x1b[xpr][:],
                          x1_s[b, trow:trow + 128, ds * 512:(ds + 1) * 512],
                          gates=[FU["x1b"][xpr], x1tok[(b, c, ds, tau)]])
                bk.gr(p, "vector")
                wt("vector", tk1)
                wt("vector", FU["x2b"][xpr])
                t_x2 = op("vector", lambda e, b_=bk, xpr=xpr:
                          e.tensor_tensor(
                    x2b[xpr][:], b_.ap[:, :], x1b[xpr][:], ALU.add),
                    inc=True)
                bk.rds.append(t_x2)
                FU["x1b"][xpr] = t_x2
                wt("gpsimd", t_x2)
                ost = dma(lout[xpr], out_t[b, trow:trow + 128,
                                           ds * 512:(ds + 1) * 512],
                          x2b[xpr][:])
                FU["x2b"][xpr] = ost
                yield 1
        FU["zuser"][zp] = zl_last

    def interleave(gens):
        live = [[g, float(w), float(d0)] for (g, w, d0) in gens]
        while live:
            g = min(live, key=lambda st: st[2] / st[1])
            try:
                inc = next(g[0])
                g[2] += max(inc, 0.25)
            except StopIteration:
                live.remove(g)

    def run_all(g):
        for _ in g:
            pass

    drain(16)
    interleave([(phaseA_gen(), 4200, 0), (prep1_gen(), 512, 30)])
    kv_mark = [(s, s.n) for s in lktl + lval]
    barrier()

    chunks = [(b, c) for b in range(B) for c in range(NCH)]
    score_w = {0: 128, 1: 256, 2: 264, 3: 392}
    for i, (b, c) in enumerate(chunks):
        aw = 256 + 256 + 2 * score_w[c] + 64
        if i == 0:
            interleave([(a_gen(b, c), aw, 0), (prep2_gen(), 330, 20)])
        else:
            pb_, pc = chunks[i - 1]
            fw = 3200
            interleave([(a_gen(b, c), aw, 0),
                        (f_gen(pb_, pc, (i - 1) % 2), fw, 0.07 * fw)])
    bl, cl = chunks[-1]
    run_all(f_gen(bl, cl, (len(chunks) - 1) % 2))

    for _s in sems.values():
        p.wait("sync", (_s, _s.n))

    with nc.Block() as blk:
        p.run(blk)
    es.close()
    return nc


def _tables():
    inv = 10000.0 ** (-(np.arange(0, HD, 2, dtype=np.float64) / HD))
    pos = np.arange(T, dtype=np.float64)
    fr = pos[None, :] * inv[:, None]
    cos32 = np.cos(fr).astype(np.float32)
    sin32 = np.sin(fr).astype(np.float32)
    cost = np.zeros((128, T), dtype=np.float32)
    sint = np.zeros((128, T), dtype=np.float32)
    for half in range(2):
        b0 = half * 64
        cost[b0:b0 + 32] = cos32
        cost[b0 + 32:b0 + 64] = cos32
        sint[b0:b0 + 32] = sin32
        sint[b0 + 32:b0 + 64] = -sin32
    qq = np.arange(128)[None, :]
    kk = np.arange(128)[:, None]
    tri = (qq >= kk).astype(np.float32).astype(BFD)
    pc = np.zeros((128, 128), dtype=np.int64)
    for i in range(128):
        for j in range(128):
            pc[i, j] = bin(i & j).count("1")
    hmat = np.where(pc % 2 == 0, 1.0, -1.0).astype(np.float32).astype(BFD)
    return cost, sint, tri, hmat


_NC_CACHE = None


def kernel(L0_attn_hidden_states, L0_attn_Wq, L0_attn_Wk, L0_attn_Wv, L0_attn_Wo,
           L0_attn_bo, L0_ffn_W1, L0_ffn_W2, L0_ffn_W3):
    global _NC_CACHE
    x = np.ascontiguousarray(L0_attn_hidden_states, dtype=np.float32)
    cost, sint, tri, hmat = _tables()
    if _NC_CACHE is None:
        _NC_CACHE = build_nc()
    nc = _NC_CACHE
    base = {
        "wq": np.ascontiguousarray(np.asarray(L0_attn_Wq, dtype=np.float32).T),
        "wk": np.ascontiguousarray(np.asarray(L0_attn_Wk, dtype=np.float32).T),
        "wv": np.ascontiguousarray(np.asarray(L0_attn_Wv, dtype=np.float32).T),
        "wo": np.ascontiguousarray(np.asarray(L0_attn_Wo, dtype=np.float32).T),
        "bo": np.ascontiguousarray(L0_attn_bo, dtype=np.float32).reshape(1, D),
        "w1": np.ascontiguousarray(np.asarray(L0_ffn_W1, dtype=np.float32).T),
        "w2": np.ascontiguousarray(np.asarray(L0_ffn_W2, dtype=np.float32).T),
        "w3": np.ascontiguousarray(np.asarray(L0_ffn_W3, dtype=np.float32).T),
        "cost_t": cost, "sint_t": sint,
        "tri_t": np.asarray(tri), "hmat_t": np.asarray(hmat),
    }
    in_maps = []
    for core in range(8):
        m = dict(base)
        m["x"] = x[core * 2:core * 2 + 2]
        in_maps.append(m)
    res = run_bass_kernel_spmd(nc, in_maps, core_ids=list(range(8)))
    return np.concatenate([r["out"] for r in res.results], axis=0)


# version 11 (final)
# speedup vs baseline: 1.1422x; 1.1083x over previous
import numpy as np
import ml_dtypes
import concourse.bass as bass
import concourse.mybir as mybir
from concourse.bass_utils import run_bass_kernel_spmd
from contextlib import ExitStack

F32 = mybir.dt.float32
BF16 = mybir.dt.bfloat16
AF = mybir.ActivationFunctionType
ALU = mybir.AluOpType
BFD = ml_dtypes.bfloat16

B, T, D, NH, HD, HID = 2, 2048, 2048, 32, 64, 8192
NTT, NCH, QC = 16, 4, 512
SHIFT = 8.0
SCL = 0.125
GSC = float(1.0 / np.sqrt(HID))

ENGS = ("sync", "tensor", "vector", "scalar", "gpsimd")


class Sem:
    __slots__ = ("h", "n", "key", "eng")

    def __init__(self, h, key):
        self.h, self.n, self.key = h, 0, key
        self.eng = None

    def bump(self, k):
        self.n += k
        return self.n


class Plan:
    def __init__(self):
        self.ops = {e: [] for e in ENGS}
        self.hwm = {}

    def emit(self, eng, fn):
        self.ops[eng].append(fn)

    def wait(self, eng, tok):
        if tok is None:
            return
        sem, val = tok
        if val <= 0:
            return
        k = (eng, sem.key)
        if self.hwm.get(k, -1) >= val:
            return
        self.hwm[k] = val
        self.emit(eng, lambda e, s=sem.h, v=val: e.wait_ge(s, v))

    def run(self, blk):
        for eng in ENGS:
            ops_ = self.ops[eng]

            def mk(o):
                def body(e):
                    for f in o:
                        f(e)
                return body
            getattr(blk, eng)(mk(ops_))


class Bank:
    def __init__(self, ap):
        self.ap = ap
        self.wr = None
        self.rds = []

    def gw(self, p, eng):
        for t in self.rds:
            p.wait(eng, t)
        self.rds = []

    def gr(self, p, eng):
        p.wait(eng, self.wr)


def build_nc():
    nc = bass.Bass("TRN2", target_bir_lowering=False)
    p = Plan()
    es = ExitStack()

    def din(name, shape, dt=F32):
        return nc.dram_tensor(name, shape, dt, kind="ExternalInput").ap()

    x_in = din("x", [B, T, D])
    w_in = {n: din(n, [D, D]) for n in ("wq", "wk", "wv", "wo")}
    bo_in = din("bo", [1, D])
    w_in["w1"] = din("w1", [D, HID])
    w_in["w2"] = din("w2", [HID, D])
    w_in["w3"] = din("w3", [D, HID])
    cost_in = din("cost_t", [128, T])
    sint_in = din("sint_t", [128, T])
    tri_in = din("tri_t", [128, 128], BF16)
    hmat_in = din("hmat_t", [128, 128], BF16)
    out_t = nc.dram_tensor("out", [B, T, D], F32, kind="ExternalOutput").ap()

    def scr(name, shape, dt=BF16):
        return nc.dram_tensor(name, shape, dt, kind="Internal").ap()

    w_bf = {n: scr(n + "_bf", [4, 128, 16 * 512])
            for n in ("wq", "wk", "wv", "wo")}
    w13img = scr("w13img", [32, 128, 2 * 16 * 256])
    w2f64 = scr("w2f64", [4, 128, 64, 512])
    w2h = scr("w2h", [4, 64, 128, 512])
    xn1_s = scr("xn1_s", [B, T, D])
    xn2_s = scr("xn2_s", [B, T, D])
    kT_s = scr("kT_s", [B, D, T])
    va_s = scr("va_s", [B, T, D])
    x1_s = scr("x1_s", [B, T, D], F32)
    z_s = scr("z_s", [2, 64, 128, 512])

    sems = {}
    for n in ("spe", "ssc", "sdv", "sgp"):
        sems[n] = Sem(es.enter_context(nc.semaphore(n)), n)

    def lane(name, eng):
        s = Sem(es.enter_context(nc.semaphore(name)), name)
        s.eng = eng
        sems[name] = s
        return s

    lts = [lane("lts0", "sync"), lane("lts1", "sync")]
    lslab = lane("lslab", "sync")
    lfsl = [lane("lfsl0", "sync"), lane("lfsl1", "sync")]
    lotl = lane("lotl", "sync")
    lw2b = [lane("lw2b0", "sync"), lane("lw2b1", "sync")]
    lzld = [lane("lzld0", "sync"), lane("lzld1", "sync")]
    lw2b4 = [lane(f"lw2b4_{i}", "sync") for i in range(4)]
    lzld4 = [lane(f"lzld4_{i}", "sync") for i in range(4)]
    lfs = [lane("lfs0", "sync"), lane("lfs1", "sync")]
    lw2h = lane("lw2h", "sync")
    lini = lane("lini", "gpsimd")
    lini2 = lane("lini2", "gpsimd")
    lxb = [lane("lxb0", "gpsimd"), lane("lxb1", "gpsimd")]
    lx1b = [lane("lx1b0", "gpsimd"), lane("lx1b1", "gpsimd")]
    lx1r = lane("lx1r", "gpsimd")
    lstx = lane("lstx", "gpsimd")
    lsty = lane("lsty", "gpsimd")
    lktl = [lane("lktl0", "gpsimd"), lane("lktl1", "gpsimd")]
    lval = [lane("lval0", "gpsimd"), lane("lval1", "gpsimd")]
    lx1s = [lane("lx1s0", "gpsimd"), lane("lx1s1", "gpsimd")]
    lout = [lane("lout0", "gpsimd"), lane("lout1", "gpsimd")]
    lkb = [lane("lkb0", "gpsimd"), lane("lkb1", "gpsimd")]
    lvb = [lane("lvb0", "gpsimd"), lane("lvb1", "gpsimd")]
    lz = [lane("lz0", "gpsimd"), lane("lz1", "gpsimd")]
    lcast = {n: lane("lc_" + n, "gpsimd")
             for n in ("wk", "wv", "wq", "wo", "w13")}
    lfld = [lane(f"lfld{i}", "gpsimd") for i in range(4)]
    spe, ssc, sdv = sems["spe"], sems["ssc"], sems["sdv"]
    ESEM = {"tensor": spe, "scalar": ssc, "vector": sdv, "gpsimd": sems["sgp"]}

    def dma(q, out_ap, in_ap, gates=()):
        for g in gates:
            p.wait(q.eng, g)
        v = q.bump(16)
        p.emit(q.eng, lambda e, o=out_ap, i=in_ap, sh=q.h:
               e.dma_start(o, i).then_inc(sh, 16))
        return (q, v)

    def dmat(q, out_ap, in_ap):
        v = q.bump(16)
        p.emit(q.eng, lambda e, o=out_ap, i=in_ap, sh=q.h:
               e.dma_start_transpose(o, i).then_inc(sh, 16))
        return (q, v)

    def op(eng, fn, inc=False):
        if inc:
            s = ESEM[eng]
            v = s.bump(1)
            p.emit(eng, lambda e, f=fn, sh=s.h: f(e).then_inc(sh, 1))
            return (s, v)
        p.emit(eng, fn)
        return None

    def wt(eng, tok):
        if isinstance(tok, list):
            for t in tok:
                p.wait(eng, t)
        else:
            p.wait(eng, tok)

    def barrier():
        toks = [(s, s.n) for s in sems.values()]
        for eng in ENGS:
            for t in toks:
                p.wait(eng, t)

    def sb(name, shape, dt=F32):
        return es.enter_context(nc.sbuf_tensor(name, shape, dt))

    cpool = sb("cpool", [128, 4])
    hmat = sb("hmat", [128, 128], BF16)
    onesb = sb("onesb", [1, 128], BF16)
    costb = sb("costb", [128, T], BF16)
    sintb = sb("sintb", [128, T], BF16)
    tri4 = sb("tri4", [128, 512], BF16)
    BO = sb("BO", [128, D])
    statb = sb("statb", [128, 64])
    xnt = [sb("xnt0", [128, 16 * 512], BF16), sb("xnt1", [128, 16 * 512],
                                                 BF16)]
    slab = sb("slab", [128, 16 * 512], BF16)
    QT = [sb(f"QT{i}", [128, 512], BF16) for i in range(2)]
    prow = [sb(f"prow{i}", [128, 512], BF16) for i in range(2)]
    ropec = sb("ropec", [128, 512], BF16)
    OT = sb("OT", [128, 16 * 512], BF16)
    ebuf = sb("ebuf", [128, 2 * 6 * 512], BF16)
    kbuf = [sb(f"kbuf{i}", [128, 2048], BF16) for i in range(2)]
    vbuf = [sb(f"vbuf{i}", [128, 16 * 130], BF16) for i in range(2)]
    o65 = [sb(f"o65_{i}", [128, 1024], BF16) for i in range(2)]
    rbf = [sb(f"rbf{i}", [1, 512], BF16) for i in range(2)]
    rbfl = [sb(f"rbfl{i}", [1, 512], F32) for i in range(2)]
    x1p = [sb(f"x1p{i}", [128, 512], F32) for i in range(2)]
    x1r = sb("x1r", [128, D], F32)
    xb = [sb(f"xb{i}", [128, 512], F32) for i in range(2)]
    xnst = sb("xnst", [128, D], BF16)
    fsl = [sb(f"fsl{i}", [128, 16 * 512], BF16) for i in range(2)]
    gbuf = [sb(f"gbuf{i}", [128, 512], BF16) for i in range(2)]
    zst = [sb(f"zst{i}", [128, 512], BF16) for i in range(2)]
    w2b = [sb(f"w2b{i}", [128, 4 * 512], BF16) for i in range(2)]
    zld = [sb(f"zld{i}", [128, 4 * 512], BF16) for i in range(2)]
    x1b = [sb(f"x1b{i}", [128, 512], F32) for i in range(2)]
    x2b = [sb(f"x2b{i}", [128, 512], F32) for i in range(2)]

    pb = [Bank(es.enter_context(nc.psum_tensor(f"pb{i}", [128, 512], F32)))
          for i in range(8)]

    U = {"xnt": [None, None], "slab": None, "xb": [None, None],
         "xnst": None, "prow": [None, None], "qt": [None, None],
         "stat": [None, None], "vstg": [None, None], "x1p": [None, None],
         "x1r": None, "kslabB": None, "vslabA": None, "vslabB": None}
    FU = {"fsl": [None, None], "gbuf": [None, None], "zst": [None, None],
          "w2b": [None, None], "zld": [None, None],
          "w2b4": [None] * 4, "zld4": [None] * 4, "x1b": [None, None],
          "x2b": [None, None], "xnt2": None, "zuser": [None, None]}
    AU = {"kbuf": [None, None], "vbuf": [None, None], "kb": [None, None],
          "vb": [None, None], "vo": [None, None],
          "ebuf": {}, "o65": [None, None], "e65": [None, None],
          "rbf": [None, None]}

    op("vector", lambda e: e.memset(cpool[:, 0:1], -SHIFT))
    op("vector", lambda e: e.memset(cpool[:, 1:2], 1e-5))
    t_ones = op("vector", lambda e: e.memset(onesb[0:1, :], 1.0), inc=True)
    nc.const_aps.register(F32, -SHIFT, cpool[:, 0:1])
    nc.const_aps.register(F32, 1e-5, cpool[:, 1:2])

    tk = dma(lini, x1r[:], cost_in)
    wt("vector", tk)
    t_cos = op("vector", lambda e: e.tensor_copy(costb[:], x1r[:]), inc=True)
    wt("gpsimd", t_cos)
    tk = dma(lini, x1r[:], sint_in)
    wt("vector", tk)
    t_sin = op("vector", lambda e: e.tensor_copy(sintb[:], x1r[:]), inc=True)
    wt("gpsimd", t_sin)
    tk = dma(lini, tri4[:, 0:128], tri_in)
    wt("vector", tk)
    t_tri = None
    for r in range(1, 4):
        wt("vector", t_tri)
        t_tri = op("vector", lambda e, r=r: e.tensor_copy(
            tri4[:, r * 128:(r + 1) * 128], tri4[:, 0:128]), inc=True)
    t_hm = dma(lini2, hmat[:], hmat_in)
    wt("gpsimd", t_tri)
    tk = dma(lini, x1r[0:1, :], bo_in)
    wt("vector", tk)
    t_bobf = op("vector", lambda e: e.tensor_copy(xnst[0:1, :], x1r[0:1, :]),
                inc=True)
    wt("tensor", t_ones)
    wt("tensor", t_bobf)
    t_BO = None
    for j in range(4):
        bk = pb[7]
        bk.gw(p, "tensor")
        tmm = op("tensor", lambda e, j=j, b_=bk: e.matmul(
            b_.ap[:, :], onesb[0:1, :], xnst[0:1, j * 512:(j + 1) * 512],
            start=True, stop=True), inc=True)
        bk.wr = tmm
        bk.gr(p, "scalar")
        t_BO = op("scalar", lambda e, j=j, b_=bk: e.copy(
            BO[:, j * 512:(j + 1) * 512], b_.ap[:, :]), inc=True)
        bk.rds.append(t_BO)
    U["xnst"] = t_BO
    U["x1r"] = t_BO

    def eoff(hh, slot):
        return (hh * 6 + slot) * 512
    stair = []
    for hh in range(2):
        for j in range(1, 4):
            stair.append(op("vector", lambda e, hh=hh, j=j: e.memset(
                ebuf[:, eoff(hh, j):eoff(hh, j) + 128 * j], 0.0), inc=True))

    cast_tok = {}
    w13tok = [None] * 32

    def cast_steps(name):
        steps = []
        for s in range(4):
            def f(s=s, name=name):
                cast_tok[name] = dma(
                    lcast[name],
                    w_bf[name][s].rearrange("q (i c) -> q i c", c=512),
                    w_in[name].rearrange("(i q) (s c) -> s q i c",
                                         q=128, c=512)[s])
            steps.append(f)
        return steps

    prep = (cast_steps("wk") + cast_steps("wv") + cast_steps("wq")
            + cast_steps("wo"))
    prep_i = {"prep": 0}

    def drain(n):
        k = 0
        while prep_i["prep"] < len(prep) and k < n:
            prep[prep_i["prep"]]()
            prep_i["prep"] += 1
            k += 1

    def prep1_gen():
        p1_ld = [None] * 4
        p1_st = [None, None]
        p1_user = [None, None]
        for q in range(128):
            sl4 = q % 4
            zb, zr = zld[sl4 // 2], (sl4 % 2) * 64
            ldk = dma(lfld[sl4], zb[zr:zr + 64, :],
                      w_in["w2"].rearrange("(t r) d -> t r d", r=128)[:, q, :],
                      gates=[p1_ld[sl4]])
            last_mm = None
            for ds in range(4):
                bk = pb[6 + (ds % 2)]
                bk.gw(p, "tensor")
                wt("tensor", ldk)
                wt("tensor", t_hm)
                tmm = op("tensor", lambda e, b_=bk, ds=ds, zb=zb, zr=zr:
                         e.matmul(
                    b_.ap[0:64, :], hmat[zr:zr + 64, 0:64],
                    zb[zr:zr + 64, ds * 512:(ds + 1) * 512],
                    start=True, stop=True, tile_position=(zr, 0)),
                    inc=True)
                bk.wr = tmm
                last_mm = tmm
                sidx = ds % 2
                eng = "scalar" if sidx == 0 else "vector"
                wt(eng, p1_user[sidx])
                bk.gr(p, eng)
                if eng == "scalar":
                    trd = op("scalar", lambda e, s=sidx, b_=bk: e.copy(
                        w2b[s][0:64, 0:512], b_.ap[0:64, :]), inc=True)
                else:
                    trd = op("vector", lambda e, s=sidx, b_=bk: e.tensor_copy(
                        w2b[s][0:64, 0:512], b_.ap[0:64, :]), inc=True)
                bk.rds.append(trd)
                stt = dma(lfs[sidx], w2f64[ds, q], w2b[sidx][0:64, 0:512],
                          gates=[trd, p1_st[sidx]])
                p1_st[sidx] = stt
                p1_user[sidx] = stt
            p1_ld[sl4] = last_mm
            FU["zld"][sl4 // 2] = last_mm
            FU["w2b"][0] = p1_st[0]
            FU["w2b"][1] = p1_st[1]
            yield 4
        FU["p1_mark"] = [(s, s.n) for s in lfs]

    def prep2_gen():
        for s in range(32):
            for w, name in ((0, "w1"), (1, "w3")):
                w13tok[s] = dma(
                    lcast["w13"],
                    w13img[s][:, w * 4096:(w + 1) * 4096].rearrange(
                        "q (i c) -> q i c", c=256),
                    w_in[name].rearrange("(i q) (s c) -> s q i c",
                                         q=128, c=256)[s])
            yield 0
        for j in range(64):
            jpr = j % 2
            ldk = dma(lzld[jpr],
                      zld[jpr][:].rearrange("q (ds c) -> q ds c", c=512),
                      w2f64[:, :, j, :].rearrange("ds q c -> q ds c"),
                      gates=[FU["zld"][jpr]] + FU["p1_mark"])
            ev = None
            for ds in range(4):
                bk = pb[4 + (ds % 2)]
                bk.gw(p, "tensor")
                wt("tensor", ldk)
                wt("tensor", t_hm)
                tmm = op("tensor", lambda e, b_=bk, ds=ds, jpr=jpr: e.matmul(
                    b_.ap[:, :], hmat[:, :],
                    zld[jpr][:, ds * 512:(ds + 1) * 512],
                    start=True, stop=True), inc=True)
                bk.wr = tmm
                FU["zld"][jpr] = tmm
                bk.gr(p, "scalar")
                wt("scalar", FU["w2b"][jpr])
                ev = op("scalar", lambda e, b_=bk, ds=ds, jpr=jpr: e.copy(
                    w2b[jpr][:, ds * 512:(ds + 1) * 512], b_.ap[:, :]),
                    inc=True)
                bk.rds.append(ev)
            stt = dma(lw2h, w2h[:, j, :, :].rearrange("ds q c -> q ds c"),
                      w2b[jpr][:].rearrange("q (ds c) -> q ds c", c=512),
                      gates=[ev])
            FU["w2b"][jpr] = stt
            yield 4
        FU["fold_mark"] = [(lw2h, lw2h.n)]

    def layer_norm(src_ap, dst_ap, parity, in_tok):
        so = 32 * parity
        wt("vector", in_tok)
        wt("vector", U["stat"][parity])
        tprev = None
        for q in range(4):
            tprev = op("vector", lambda e, q=q, so=so, s_=src_ap: e.bn_stats(
                statb[:, so + 6 * q:so + 6 * q + 6],
                s_[:, q * 512:(q + 1) * 512]), inc=(q == 3))
        wt("vector", tprev)
        t_ag = op("vector", lambda e, so=so: e.bn_aggr(
            statb[:, so + 24:so + 26], statb[:, so + 0:so + 24]), inc=True)
        wt("scalar", t_ag)
        t_ln = op("scalar", lambda e, so=so: e.activation(
            statb[:, so + 26:so + 27], statb[:, so + 25:so + 26], AF.Ln,
            bias=1e-5), inc=True)
        wt("scalar", t_ln)
        t_rs = op("scalar", lambda e, so=so: e.activation(
            statb[:, so + 27:so + 28], statb[:, so + 26:so + 27], AF.Exp,
            scale=-0.5), inc=True)
        wt("vector", t_rs)
        t_mr = op("vector", lambda e, so=so: e.tensor_tensor(
            statb[:, so + 28:so + 29], statb[:, so + 24:so + 25],
            statb[:, so + 27:so + 28], ALU.mult), inc=True)
        wt("vector", t_mr)
        t_xn = op("vector", lambda e, so=so, s_=src_ap, d_=dst_ap:
                  e.tensor_scalar(d_, s_, statb[:, so + 27:so + 28],
                                  statb[:, so + 28:so + 29], ALU.mult,
                                  ALU.subtract), inc=True)
        U["stat"][parity] = t_xn
        return t_xn

    xn1_done = {0: None, 1: None}

    def ln1_tile(b, i):
        tk = dma(lx1r, x1r[:], x_in[b, i * 128:(i + 1) * 128, :],
                 gates=[U["x1r"]])
        wt("vector", U["xnst"])
        t_xn = layer_norm(x1r[:], xnst[:], i % 2, tk)
        U["x1r"] = t_xn
        wt("gpsimd", t_xn)
        last = dma(lstx, xn1_s[b, i * 128:(i + 1) * 128, :], xnst[:])
        U["xnst"] = last
        if i == NTT - 1:
            xn1_done[b] = [last]

    def load_xnt(bi, lane_, src, b, c, pre):
        wt(lane_.eng, U["xnt"][bi])
        wt(lane_.eng, pre)
        tok = None
        for i in range(16):
            tok = dmat(lane_, xnt[bi][:, i * 512:(i + 1) * 512],
                       src[b, c * 512:(c + 1) * 512, i * 128:(i + 1) * 128])
        return tok

    def load_slab(name, s):
        return dma(lslab, slab[:], w_bf[name][s],
                   gates=[U["slab"], cast_tok[name]])

    def rope(ppr, col0, t_ev):
        rt = prow[ppr]
        wt("vector", t_ev)
        wt("vector", t_cos)
        wt("vector", t_sin)
        t4 = None
        for (d0, s0) in ((0, 32), (32, 0), (64, 96), (96, 64)):
            t4 = op("vector", lambda e, d0=d0, s0=s0, rt=rt, col0=col0:
                    e.tensor_tensor(ropec[d0:d0 + 32, :], rt[s0:s0 + 32, :],
                                    sintb[s0:s0 + 32, col0:col0 + QC],
                                    ALU.mult), inc=(d0 == 96))
        t_qc = op("vector", lambda e, rt=rt, col0=col0: e.tensor_tensor(
            rt[:], rt[:], costb[:, col0:col0 + QC], ALU.mult), inc=True)
        wt("vector", t4)
        wt("vector", t_qc)
        return op("vector", lambda e, rt=rt: e.tensor_tensor(
            rt[:], rt[:], ropec[:], ALU.add), inc=True)

    XKP = {"a": None, "f": None}

    def kv_chunk(b, c, xk, nxt):
        cpr = c % 2
        for s in range(4):
            if s % 2 == 0:
                kbuf_w, klane = slab, lslab
                sl = dma(lslab, slab[:], w_bf["wk"][s],
                         gates=[U["slab"], cast_tok["wk"]])
                U_key = "slab"
            else:
                kbuf_w, klane = fsl[1], lfsl[1]
                sl = dma(lfsl[1], fsl[1][:], w_bf["wk"][s],
                         gates=[U["kslabB"], cast_tok["wk"]])
                U_key = "kslabB"
            for J in range(4 * s, 4 * s + 4):
                bk = pb[J % 2]
                bk.gw(p, "tensor")
                wt("tensor", xk)
                wt("tensor", sl)
                tmm = None
                for i in range(16):
                    tmm = op("tensor", lambda e, b_=bk, i=i, J=J, s=s,
                             cpr=cpr, w_=kbuf_w: e.matmul(
                        b_.ap[:, :],
                        w_[:, i * 512 + (J - 4 * s) * 128:
                           i * 512 + (J - 4 * s) * 128 + 128],
                        xnt[cpr][:, i * 512:(i + 1) * 512],
                        start=(i == 0), stop=(i == 15)), inc=(i == 15))
                bk.wr = tmm
                U[U_key] = tmm
                U["xnt"][cpr] = tmm
                ppr = J % 2
                bk.gr(p, "scalar")
                wt("scalar", U["prow"][ppr])
                t_ev = op("scalar", lambda e, b_=bk, ppr=ppr: e.copy(
                    prow[ppr][:], b_.ap[:, :]), inc=True)
                bk.rds.append(t_ev)
                t_rp = rope(ppr, c * QC, t_ev)
                wt("gpsimd", t_rp)
                k_last = dma(lktl[ppr], kT_s[b, J * 128:(J + 1) * 128,
                                             c * QC:(c + 1) * QC],
                             prow[ppr][:])
                U["prow"][ppr] = k_last
                yield 16
        if nxt is not None:
            nb, nxc = nxt
            XKP["a"] = load_xnt(nxc % 2, lts[nxc % 2], xn1_s, nb, nxc,
                                xn1_done[nb])
        for s in range(4):
            if s % 2 == 0:
                vbuf_w = fsl[0]
                sl = dma(lfsl[0], fsl[0][:], w_bf["wv"][s],
                         gates=[U["vslabA"], cast_tok["wv"]])
                U_key = "vslabA"
            else:
                vbuf_w = OT
                sl = dma(lotl, OT[:], w_bf["wv"][s],
                         gates=[U["vslabB"], cast_tok["wv"]])
                U_key = "vslabB"
            for tt in range(4):
                bk = pb[2 + tt % 2]
                bk.gw(p, "tensor")
                wt("tensor", xk)
                wt("tensor", sl)
                tmm = None
                for i in range(16):
                    tmm = op("tensor", lambda e, b_=bk, i=i, tt=tt,
                             cpr=cpr, w_=vbuf_w: e.matmul(
                        b_.ap[:, :],
                        xnt[cpr][:, i * 512 + tt * 128:
                                  i * 512 + tt * 128 + 128],
                        w_[:, i * 512:(i + 1) * 512],
                        start=(i == 0), stop=(i == 15)), inc=(i == 15))
                bk.wr = tmm
                U[U_key] = tmm
                U["xnt"][cpr] = tmm
                vpr = tt % 2
                bk.gr(p, "scalar")
                wt("scalar", U["vstg"][vpr])
                t_ev = op("scalar", lambda e, b_=bk, vpr=vpr: e.copy(
                    zst[vpr][:], b_.ap[:, :]), inc=True)
                bk.rds.append(t_ev)
                wt("gpsimd", t_ev)
                v_last = dma(lval[vpr], va_s[b, (c * 4 + tt) * 128:
                                             (c * 4 + tt) * 128 + 128,
                                             s * 512:(s + 1) * 512],
                             zst[vpr][:])
                U["vstg"][vpr] = v_last
                FU["zst"][vpr] = v_last
                yield 16

    def phaseA_gen():
        for i in range(NTT):
            ln1_tile(0, i)
            yield 0.5
        seq = [(0, c) for c in range(NCH)] + [(1, c) for c in range(NCH)]
        xk = load_xnt(0, lts[0], xn1_s, 0, 0, xn1_done[0])
        for idx, (b, c) in enumerate(seq):
            nxt = seq[idx + 1] if idx + 1 < len(seq) else None
            yield from kv_chunk(b, c, xk, nxt)
            xk = XKP["a"]
            if b == 0 and c < 2:
                for i in range(8):
                    ln1_tile(1, 8 * c + i)
                    yield 0.5

    kv_mark = None

    def emit_kv_loads(b, c, hp):
        q0 = c * QC
        hpr = hp % 2
        glob = (hp == 15)
        kmin = 0 if (glob or c < 2) else 512
        kend = q0 + 512
        nkt = (kend - kmin) // 128
        kb = dma(lkb[hpr], kbuf[hpr][:, 0:kend - kmin],
                 kT_s[b, hp * 128:(hp + 1) * 128, kmin:kend],
                 gates=[AU["kbuf"][hpr]] + kv_mark)
        wt("gpsimd", AU["vbuf"][hpr])
        vb = None
        for m in range(2):
            vb = dma(lvb[hpr], vbuf[hpr][:, 0:nkt * 130]
                     .rearrange("q (n h c) -> q n h c", h=2, c=65)
                     [:, :, m, 0:64],
                     va_s[b, kmin:kend,
                          hp * 128 + m * 64:hp * 128 + m * 64 + 64]
                     .rearrange("(n q) c -> q n c", q=128))
        wt("vector", vb)
        t_vo = op("vector", lambda e, hpr=hpr, nkt=nkt: e.memset(
            vbuf[hpr][:, 0:nkt * 130]
            .rearrange("q (n h c) -> q n h c", h=2, c=65)[:, :, :, 64:65],
            1.0), inc=True)
        AU["kb"][hpr] = kb
        AU["vb"][hpr] = vb
        AU["vo"][hpr] = t_vo

    ot_tok = {}
    x1tok = {}
    xn2tok = {}

    def emit_norm(hp, hpr):
        t_ot = None
        wt("vector", AU["e65"][hpr])
        wt("scalar", AU["e65"][hpr])
        for hh in range(2):
            col0 = hh * 512
            wt("scalar", AU["rbf"][hh])
            t_lg = op("scalar", lambda e, hh=hh, hpr=hpr, col0=col0:
                      e.activation(rbfl[hh][:],
                                   o65[hpr][64:65, col0:col0 + 512],
                                   AF.Ln), inc=True)
            wt("scalar", t_lg)
            t_rc = op("scalar", lambda e, hh=hh: e.activation(
                rbf[hh][:], rbfl[hh][:], AF.Exp, scale=-1.0), inc=True)
            rbk = pb[hh]
            rbk.gw(p, "tensor")
            wt("tensor", t_rc)
            wt("tensor", t_ones)
            t_rb = op("tensor", lambda e, b_=rbk, hh=hh: e.matmul(
                b_.ap[0:64, :], onesb[0:1, 0:64], rbf[hh][:],
                start=True, stop=True), inc=True)
            rbk.wr = t_rb
            AU["rbf"][hh] = t_rb
            rbk.gr(p, "vector")
            t_ot = op("vector", lambda e, b_=rbk, hh=hh, hp=hp, hpr=hpr,
                      col0=col0: e.tensor_tensor(
                OT[hh * 64:hh * 64 + 64, hp * 512:(hp + 1) * 512],
                o65[hpr][0:64, col0:col0 + 512], b_.ap[0:64, :], ALU.mult),
                inc=True)
            rbk.rds.append(t_ot)
        AU["o65"][hpr] = t_ot
        ot_tok[hp] = t_ot

    def score_av(b, c, hp, qpr):
        q0 = c * QC
        hpr = hp % 2
        glob = (hp == 15)
        kmin = 0 if (glob or c < 2) else 512
        kend = q0 + 512
        full_k0 = list(range(kmin, q0, 128))
        stra_k0 = [q0 + 128 * j for j in range(4)]
        kb = AU["kb"][hpr]
        vb = AU["vb"][hpr]
        t_vo = AU["vo"][hpr]
        qt_tok = U["qt"][qpr]

        exp_tok = {}
        for j, k0 in enumerate(stra_k0):
            for hh in range(2):
                bk = pb[hh]
                bk.gw(p, "tensor")
                wt("tensor", kb)
                wt("tensor", qt_tok)
                tmm = op("tensor", lambda e, b_=bk, hh=hh, k0=k0, hpr=hpr,
                         kmin=kmin, qpr=qpr: e.matmul(
                    b_.ap[:, :],
                    kbuf[hpr][hh * 64:hh * 64 + 64,
                              k0 - kmin:k0 - kmin + 128],
                    QT[qpr][hh * 64:hh * 64 + 64, :],
                    start=True, stop=True, tile_position=(hh * 64, 0)),
                    inc=True)
                bk.wr = tmm
                bk.gr(p, "scalar")
                wt("scalar", AU["ebuf"].get((hh, j)))
                t_ex = op("scalar", lambda e, b_=bk, hh=hh, j=j:
                          e.activation(
                    ebuf[:, eoff(hh, j) + 128 * j:eoff(hh, j) + 512],
                    b_.ap[:, 128 * j:512], AF.Exp, scale=SCL, bias=-SHIFT),
                    inc=True)
                bk.rds.append(t_ex)
                exp_tok[(hh, j)] = t_ex
            yield 2
        band = []
        for hh in range(2):
            for j in range(4):
                wt("vector", exp_tok[(hh, j)])
            wt("vector", t_tri)
            for tz in stair:
                wt("vector", tz)
            band.append(op("vector", lambda e, hh=hh: e.tensor_tensor(
                ebuf[:, eoff(hh, 0):eoff(hh, 0) + 4 * 640]
                .rearrange("q (j c) -> q j c", c=640)[:, :, 0:128],
                ebuf[:, eoff(hh, 0):eoff(hh, 0) + 4 * 640]
                .rearrange("q (j c) -> q j c", c=640)[:, :, 0:128],
                tri4[:].rearrange("q (j c) -> q j c", c=128), ALU.mult),
                inc=True))
        yield 0

        seq = [("f", k0) for k0 in full_k0] + [("s", k0) for k0 in stra_k0]
        o_tok = [None, None]
        for idx, (kind, k0) in enumerate(seq):
            first = (idx == 0)
            lastt = (idx == len(seq) - 1)
            if kind == "f":
                slot = 4 + (idx % 2)
                for hh in range(2):
                    bk = pb[hh]
                    bk.gw(p, "tensor")
                    wt("tensor", kb)
                    wt("tensor", qt_tok)
                    tmm = op("tensor", lambda e, b_=bk, hh=hh, k0=k0,
                             hpr=hpr, kmin=kmin, qpr=qpr: e.matmul(
                        b_.ap[:, :],
                        kbuf[hpr][hh * 64:hh * 64 + 64,
                                  k0 - kmin:k0 - kmin + 128],
                        QT[qpr][hh * 64:hh * 64 + 64, :],
                        start=True, stop=True, tile_position=(hh * 64, 0)),
                        inc=True)
                    bk.wr = tmm
                    bk.gr(p, "scalar")
                    wt("scalar", AU["ebuf"].get((hh, slot)))
                    t_ex = op("scalar", lambda e, b_=bk, hh=hh, slot=slot:
                              e.activation(
                        ebuf[:, eoff(hh, slot):eoff(hh, slot) + 512],
                        b_.ap[:, :], AF.Exp, scale=SCL, bias=-SHIFT),
                        inc=True)
                    bk.rds.append(t_ex)
                    exp_tok[(hh, slot)] = t_ex
            else:
                slot = (k0 - q0) // 128
            for hh in range(2):
                obk = pb[2 + hh]
                if first:
                    obk.gw(p, "tensor")
                if kind == "f":
                    wt("tensor", exp_tok[(hh, slot)])
                else:
                    wt("tensor", band[hh])
                wt("tensor", vb)
                wt("tensor", t_vo)
                kti = (k0 - kmin) // 128
                tmo = op("tensor", lambda e, b_=obk, hh=hh, kti=kti,
                         hpr=hpr, slot=slot, first=first, lastt=lastt:
                         e.matmul(
                    b_.ap[0:65, :],
                    vbuf[hpr][:, kti * 130 + hh * 65:
                              kti * 130 + hh * 65 + 65],
                    ebuf[:, eoff(hh, slot):eoff(hh, slot) + 512],
                    start=first, stop=lastt), inc=True)
                o_tok[hh] = tmo
                if kind == "f":
                    AU["ebuf"][(hh, slot)] = tmo
            yield 4 if kind == "f" else 2
        pb[2].wr = o_tok[0]
        pb[3].wr = o_tok[1]
        AU["kbuf"][hpr] = o_tok[1]
        AU["vbuf"][hpr] = o_tok[1]
        for j in range(4):
            AU["ebuf"][(0, j)] = o_tok[0]
            AU["ebuf"][(1, j)] = o_tok[1]
        U["qt"][qpr] = o_tok[1]

        wt("scalar", AU["o65"][hpr])
        t_e65 = None
        for hh in range(2):
            bk = pb[2 + hh]
            bk.gr(p, "scalar")
            t_e65 = op("scalar", lambda e, b_=bk, hh=hh, hpr=hpr: e.copy(
                o65[hpr][0:65, hh * 512:hh * 512 + 512], b_.ap[0:65, :]),
                inc=True)
            bk.rds.append(t_e65)
        AU["e65"][hpr] = t_e65
        yield 1

    def a_gen(b, c, nxt=None):
        if XKP["f"] is not None:
            xk = XKP["f"]
            XKP["f"] = None
        else:
            xk = load_xnt(0, lts[0], xn1_s, b, c, xn1_done[b])
        yield 1
        for s in range(4):
            sl = load_slab("wq", s)
            for J in range(4 * s, 4 * s + 4):
                hp = J
                emit_kv_loads(b, c, hp)
                qpr = J % 2
                bk = pb[qpr]
                bk.gw(p, "tensor")
                wt("tensor", xk)
                wt("tensor", sl)
                tmm = None
                for i in range(16):
                    tmm = op("tensor", lambda e, b_=bk, i=i, J=J, s=s:
                             e.matmul(
                        b_.ap[:, :],
                        slab[:, i * 512 + (J - 4 * s) * 128:
                             i * 512 + (J - 4 * s) * 128 + 128],
                        xnt[0][:, i * 512:(i + 1) * 512],
                        start=(i == 0), stop=(i == 15)), inc=(i == 15))
                bk.wr = tmm
                U["slab"] = tmm
                U["xnt"][0] = tmm
                bk.gr(p, "scalar")
                wt("scalar", U["prow"][qpr])
                t_ev = op("scalar", lambda e, b_=bk, qpr=qpr: e.copy(
                    prow[qpr][:], b_.ap[:, :]), inc=True)
                bk.rds.append(t_ev)
                t_rp = rope(qpr, c * QC, t_ev)
                wt("vector", t_rp)
                wt("vector", U["qt"][qpr])
                t_cp = op("vector", lambda e, qpr=qpr: e.tensor_copy(
                    QT[qpr][:], prow[qpr][:]), inc=True)
                U["prow"][qpr] = t_cp
                U["qt"][qpr] = t_cp
                if hp > 0:
                    emit_norm(hp - 1, (hp - 1) % 2)
                yield 18
                yield from score_av(b, c, hp, qpr)
        emit_norm(15, 1)
        if nxt is not None:
            XKP["f"] = load_xnt(0, lts[0], xn1_s, nxt[0], nxt[1],
                                xn1_done[nxt[0]])
        yield 2
        for s in range(4):
            sl = load_slab("wo", s)
            for tau in range(4):
                trow = c * 512 + tau * 128
                bk = pb[tau % 2]
                bk.gw(p, "tensor")
                wt("tensor", sl)
                tmm = None
                for i in range(16):
                    wt("tensor", ot_tok[i])
                    tmm = op("tensor", lambda e, b_=bk, i=i, tau=tau:
                             e.matmul(
                        b_.ap[:, :],
                        OT[:, i * 512 + tau * 128:i * 512 + tau * 128 + 128],
                        slab[:, i * 512:(i + 1) * 512],
                        start=(i == 0), stop=(i == 15)), inc=(i == 15))
                bk.wr = tmm
                U["slab"] = tmm
                xpr = tau % 2
                tkx = dma(lxb[xpr], xb[xpr][:],
                          x_in[b, trow:trow + 128, s * 512:(s + 1) * 512],
                          gates=[U["xb"][xpr]])
                bk.gr(p, "vector")
                wt("vector", tkx)
                wt("vector", t_BO)
                wt("vector", U["x1p"][xpr])
                t_a1 = op("vector", lambda e, b_=bk, s=s, xpr=xpr:
                          e.tensor_tensor(
                    x1p[xpr][:], b_.ap[:, :], BO[:, s * 512:(s + 1) * 512],
                    ALU.add), inc=True)
                wt("vector", t_a1)
                t_a2 = op("vector", lambda e, xpr=xpr: e.tensor_tensor(
                    x1p[xpr][:], x1p[xpr][:], xb[xpr][:], ALU.add),
                    inc=True)
                bk.rds.append(t_a2)
                U["xb"][xpr] = t_a2
                wt("gpsimd", t_a2)
                stx = dma(lx1s[xpr],
                          x1_s[b, trow:trow + 128, s * 512:(s + 1) * 512],
                          x1p[xpr][:])
                U["x1p"][xpr] = stx
                x1tok[(b, c, s, tau)] = stx
                yield 16
        toks = []
        for tau in range(4):
            trow = c * 512 + tau * 128
            tk = dma(lx1r, x1r[:], x1_s[b, trow:trow + 128, :],
                     gates=[U["x1r"]] + [x1tok[(b, c, s, tau)]
                                         for s in range(4)])
            wt("vector", U["xnst"])
            t_xn = layer_norm(x1r[:], xnst[:], tau % 2, tk)
            U["x1r"] = t_xn
            wt("gpsimd", t_xn)
            st = dma(lsty, xn2_s[b, trow:trow + 128, :], xnst[:])
            U["xnst"] = st
            toks.append(st)
            yield 1
        xn2tok[(b, c)] = toks

    def f_gen(b, c, zp):
        wt("sync", FU["xnt2"])
        wt("sync", xn2tok[(b, c)])
        xk = None
        for i in range(16):
            xk = dmat(lts[1], xnt[1][:, i * 512:(i + 1) * 512],
                      xn2_s[b, c * 512:(c + 1) * 512,
                            i * 128:(i + 1) * 128])
        yield 1
        ztok = [None] * 64
        wt("gpsimd", FU["zuser"][zp])
        for s in range(32):
            spr = s % 2
            tw = dma(lfsl[spr], fsl[spr][:], w13img[s],
                     gates=[FU["fsl"][spr], w13tok[s]])
            for Jl in range(2):
                J = 2 * s + Jl
                bkA = pb[4 + 2 * (J % 2)]
                bkB = pb[5 + 2 * (J % 2)]
                for bk, woff in ((bkA, 0), (bkB, 4096)):
                    bk.gw(p, "tensor")
                    wt("tensor", xk)
                    wt("tensor", tw)
                    tmm = None
                    for i in range(16):
                        tmm = op("tensor", lambda e, b_=bk, i=i, Jl=Jl,
                                 spr=spr, woff=woff: e.matmul(
                            b_.ap[:, :],
                            fsl[spr][:, woff + i * 256 + Jl * 128:
                                     woff + i * 256 + Jl * 128 + 128],
                            xnt[1][:, i * 512:(i + 1) * 512],
                            start=(i == 0), stop=(i == 15)),
                            inc=(i == 15))
                    bk.wr = tmm
                    FU["fsl"][spr] = tmm
                    FU["xnt2"] = tmm
                gpr = J % 2
                bkA.gr(p, "scalar")
                wt("scalar", FU["gbuf"][gpr])
                t_si = op("scalar", lambda e, b_=bkA, gpr=gpr:
                          e.activation(gbuf[gpr][:], b_.ap[:, :], AF.Silu),
                          inc=True)
                bkA.rds.append(t_si)
                bkB.gr(p, "vector")
                wt("vector", t_si)
                wt("vector", FU["zst"][gpr])
                t_g = op("vector", lambda e, b_=bkB, gpr=gpr:
                         e.scalar_tensor_tensor(
                    zst[gpr][:], b_.ap[:, :], GSC, gbuf[gpr][:],
                    ALU.mult, ALU.mult), inc=True)
                bkB.rds.append(t_g)
                FU["gbuf"][gpr] = t_g
                wt("gpsimd", t_g)
                stz = dma(lz[gpr], z_s[zp, J], zst[gpr][:])
                FU["zst"][gpr] = stz
                ztok[J] = stz
                yield 32
        zl_last = None
        for ds in range(4):
            for tau in range(4):
                pb[4 + tau].gw(p, "tensor")
            for g2 in range(32):
                bi = g2 % 4
                hof = (bi % 2) * 1024
                wg = FU["w2b4"][bi] or FU["w2b"][bi // 2]
                tw2 = dma(lw2b4[bi],
                          w2b[bi // 2][:, hof:hof + 1024].rearrange(
                              "q (j c) -> q j c", c=512),
                          w2h[ds, 2 * g2:2 * g2 + 2].rearrange(
                              "j q c -> q j c"),
                          gates=[wg] + FU["fold_mark"])
                zg = FU["zld4"][bi] or FU["zld"][bi // 2]
                tzl = dma(lzld4[bi],
                          zld[bi // 2][:, hof:hof + 1024].rearrange(
                              "q (j c) -> q j c", c=512),
                          z_s[zp, 2 * g2:2 * g2 + 2].rearrange(
                              "j q c -> q j c"),
                          gates=[zg, ztok[2 * g2], ztok[2 * g2 + 1]])
                zl_last = tzl
                last4 = [None] * 4
                for J2 in range(2):
                    for tau in range(4):
                        bk = pb[4 + tau]
                        wt("tensor", tw2)
                        wt("tensor", tzl)
                        tmm = op("tensor", lambda e, b_=bk, J2=J2, tau=tau,
                                 bi=bi, hof=hof, g2=g2: e.matmul(
                            b_.ap[:, :],
                            zld[bi // 2][:, hof + J2 * 512 + tau * 128:
                                         hof + J2 * 512 + tau * 128 + 128],
                            w2b[bi // 2][:, hof + J2 * 512:
                                         hof + (J2 + 1) * 512],
                            start=(g2 == 0 and J2 == 0),
                            stop=(g2 == 31 and J2 == 1)),
                            inc=(J2 == 1))
                        last4[tau] = tmm
                FU["w2b4"][bi] = last4[3]
                FU["zld4"][bi] = last4[3]
                yield 8
            for tau in range(4):
                bk = pb[4 + tau]
                bk.wr = last4[tau]
                trow = c * 512 + tau * 128
                xpr = tau % 2
                tk1 = dma(lx1b[xpr], x1b[xpr][:],
                          x1_s[b, trow:trow + 128, ds * 512:(ds + 1) * 512],
                          gates=[FU["x1b"][xpr], x1tok[(b, c, ds, tau)]])
                bk.gr(p, "vector")
                wt("vector", tk1)
                wt("vector", FU["x2b"][xpr])
                t_x2 = op("vector", lambda e, b_=bk, xpr=xpr:
                          e.tensor_tensor(
                    x2b[xpr][:], b_.ap[:, :], x1b[xpr][:], ALU.add),
                    inc=True)
                bk.rds.append(t_x2)
                FU["x1b"][xpr] = t_x2
                wt("gpsimd", t_x2)
                ost = dma(lout[xpr], out_t[b, trow:trow + 128,
                                           ds * 512:(ds + 1) * 512],
                          x2b[xpr][:])
                FU["x2b"][xpr] = ost
                yield 1
        FU["zuser"][zp] = zl_last

    def interleave(gens):
        live = [[g, float(w), float(d0)] for (g, w, d0) in gens]
        while live:
            g = min(live, key=lambda st: st[2] / st[1])
            try:
                inc = next(g[0])
                g[2] += max(inc, 0.25)
            except StopIteration:
                live.remove(g)

    def run_all(g):
        for _ in g:
            pass

    drain(16)
    interleave([(phaseA_gen(), 4200, 0), (prep1_gen(), 512, 30)])
    kv_mark = [(s, s.n) for s in lktl + lval]
    barrier()

    chunks = [(b, c) for b in range(B) for c in range(NCH)]
    score_w = {0: 128, 1: 256, 2: 264, 3: 392}
    for i, (b, c) in enumerate(chunks):
        aw = 256 + 256 + 2 * score_w[c] + 64
        nxt = chunks[i + 1] if i + 1 < len(chunks) else None
        if i == 0:
            interleave([(a_gen(b, c, nxt), aw, 0), (prep2_gen(), 330, 20)])
        else:
            pb_, pc = chunks[i - 1]
            fw = 3200
            interleave([(a_gen(b, c, nxt), aw, 0),
                        (f_gen(pb_, pc, (i - 1) % 2), fw, 0.07 * fw)])
    bl, cl = chunks[-1]
    run_all(f_gen(bl, cl, (len(chunks) - 1) % 2))

    for _s in sems.values():
        p.wait("sync", (_s, _s.n))

    with nc.Block() as blk:
        p.run(blk)
    es.close()
    return nc


def _tables():
    inv = 10000.0 ** (-(np.arange(0, HD, 2, dtype=np.float64) / HD))
    pos = np.arange(T, dtype=np.float64)
    fr = pos[None, :] * inv[:, None]
    cos32 = np.cos(fr).astype(np.float32)
    sin32 = np.sin(fr).astype(np.float32)
    cost = np.zeros((128, T), dtype=np.float32)
    sint = np.zeros((128, T), dtype=np.float32)
    for half in range(2):
        b0 = half * 64
        cost[b0:b0 + 32] = cos32
        cost[b0 + 32:b0 + 64] = cos32
        sint[b0:b0 + 32] = sin32
        sint[b0 + 32:b0 + 64] = -sin32
    qq = np.arange(128)[None, :]
    kk = np.arange(128)[:, None]
    tri = (qq >= kk).astype(np.float32).astype(BFD)
    pc = np.zeros((128, 128), dtype=np.int64)
    for i in range(128):
        for j in range(128):
            pc[i, j] = bin(i & j).count("1")
    hmat = np.where(pc % 2 == 0, 1.0, -1.0).astype(np.float32).astype(BFD)
    return cost, sint, tri, hmat


_NC_CACHE = None


def kernel(L0_attn_hidden_states, L0_attn_Wq, L0_attn_Wk, L0_attn_Wv, L0_attn_Wo,
           L0_attn_bo, L0_ffn_W1, L0_ffn_W2, L0_ffn_W3):
    global _NC_CACHE
    x = np.ascontiguousarray(L0_attn_hidden_states, dtype=np.float32)
    cost, sint, tri, hmat = _tables()
    if _NC_CACHE is None:
        _NC_CACHE = build_nc()
    nc = _NC_CACHE
    base = {
        "wq": np.ascontiguousarray(np.asarray(L0_attn_Wq, dtype=np.float32).T),
        "wk": np.ascontiguousarray(np.asarray(L0_attn_Wk, dtype=np.float32).T),
        "wv": np.ascontiguousarray(np.asarray(L0_attn_Wv, dtype=np.float32).T),
        "wo": np.ascontiguousarray(np.asarray(L0_attn_Wo, dtype=np.float32).T),
        "bo": np.ascontiguousarray(L0_attn_bo, dtype=np.float32).reshape(1, D),
        "w1": np.ascontiguousarray(np.asarray(L0_ffn_W1, dtype=np.float32).T),
        "w2": np.ascontiguousarray(np.asarray(L0_ffn_W2, dtype=np.float32).T),
        "w3": np.ascontiguousarray(np.asarray(L0_ffn_W3, dtype=np.float32).T),
        "cost_t": cost, "sint_t": sint,
        "tri_t": np.asarray(tri), "hmat_t": np.asarray(hmat),
    }
    in_maps = []
    for core in range(8):
        m = dict(base)
        m["x"] = x[core * 2:core * 2 + 2]
        in_maps.append(m)
    res = run_bass_kernel_spmd(nc, in_maps, core_ids=list(range(8)))
    return np.concatenate([r["out"] for r in res.results], axis=0)
